# version 03
# baseline (speedup 1.0000x reference)
import numpy as np
from contextlib import ExitStack

import concourse.bass as bass
import concourse.mybir as mybir
from concourse.bass_utils import run_bass_kernel_spmd

F32 = mybir.dt.float32
BF16 = mybir.dt.bfloat16
AF = mybir.ActivationFunctionType
ALU = mybir.AluOpType
AX = mybir.AxisListType

B, L, DM = 4, 8192, 1024
DI, DS = 2048, 64
IF = 2816
SPAN, SEG = 4096, 512
NSEG = SPAN // SEG
MIDS = [0.25, 0.75, 1.25, 1.75, 2.5, 3.5, 5.0]

ENGS = ["sync", "vector", "scalar", "tensor", "gpsimd"]
SEMS = ["d1", "d2", "ve", "se", "pe", "cc", "st"]


class Plan:
    def __init__(self):
        self.steps = []
        self.cnt = {s: 0 for s in SEMS}
        self.seen = {e: {s: 0 for s in SEMS} for e in ENGS}
        self.sem = {}

    def step(self, eng, fn, deps=(), inc=None, n=1):
        if eng in ("vector", "scalar", "tensor"):
            deps = tuple(deps) + ("d1", "d2")
        waits = []
        for s in deps:
            v = self.cnt[s]
            if v > self.seen[eng][s]:
                waits.append((s, v))
                self.seen[eng][s] = v
        if inc is not None:
            self.cnt[inc] += n
        sem = self.sem

        def run(e, waits=waits, fn=fn, inc=inc, n=n):
            for s, v in waits:
                e.wait_ge(sem[s], v)
            ins = fn(e)
            if inc is not None:
                ins.then_inc(sem[inc], n)
        self.steps.append((eng, run))


P = Plan()


def dma1(o, i, deps=("ve", "se", "pe"), **kw):
    P.step("sync", lambda e, o=o, i=i, kw=kw: e.dma_start(out=o, in_=i, **kw),
           deps=deps, inc="d1", n=16)


def dma1t(o, i, deps=("d1",)):
    P.step("sync", lambda e, o=o, i=i: e.dma_start(out=o, in_=i, transpose=True),
           deps=deps, inc="d1", n=16)


def dma2(o, i, deps=("ve", "se", "pe"), **kw):
    P.step("gpsimd", lambda e, o=o, i=i, kw=kw: e.dma_start(out=o, in_=i, **kw),
           deps=deps, inc="d2", n=16)


def ve(fn, deps=(), drain=True):
    P.step("vector", fn, deps=deps, inc="ve")
    if drain:
        P.step("vector", lambda e: e.drain())


def se(fn, deps=(), drain=True):
    P.step("scalar", fn, deps=deps, inc="se")
    if drain:
        P.step("scalar", lambda e: e.drain())


def pe(fn, deps=()):
    P.step("tensor", fn, deps=deps, inc="pe")


def barrier():
    allsems = ("d1", "d2", "ve", "se", "pe", "cc")
    for eng in ENGS:
        P.step(eng, lambda e: e.nop(), deps=allsems)


def build_program():
    global P
    P = Plan()
    nc = bass.Bass("TRN2", target_bir_lowering=False)

    def din(name, shape):
        return nc.dram_tensor(name, shape, F32, kind="ExternalInput").ap()

    xT = din("xT", [DM, SPAN + 3])
    xtm = din("xtm", [SPAN, DM])
    winT = din("winT", [DM, 2 * DI])
    wxT = din("wxT", [DI, 129])
    convd = din("convd", [128, 64 * 128])
    convb = din("convb", [128, 16])
    alog = din("alog", [1, DS])
    dcol = din("dcol", [128, 16])
    woutT = din("woutT", [DI, DM])
    wg = din("wg", [IF, DM])
    wu = din("wu", [IF, DM])
    wd = din("wd", [DM, IF])
    maskrow = din("maskrow", [1, 1024])
    slotsel = din("slotsel", [64, 4])
    recvsel = din("recvsel", [64, 4])
    out = nc.dram_tensor("out", [SPAN, DM], F32, kind="ExternalOutput").ap()
    dbg = nc.dram_tensor("dbg", [128, 4096], BF16, kind="ExternalOutput").ap()

    def dsc(name, shape, dt=BF16, **kw):
        return nc.dram_tensor(name, shape, dt, kind="Internal", **kw)

    sc_xc = dsc("sc_xc", [DI, SPAN]).ap()
    sc_z = dsc("sc_z", [DI, SPAN]).ap()
    sc_y = dsc("sc_y", [SPAN, DI]).ap()
    sc_xq = dsc("sc_xq", [SPAN, DM]).ap()
    sc_hq = dsc("sc_hq", [SPAN, IF]).ap()
    sc_wgq = dsc("sc_wgq", [IF, DM]).ap()
    sc_wuq = dsc("sc_wuq", [IF, DM]).ap()
    sc_wdq = dsc("sc_wdq", [DM, IF]).ap()
    sc_M = dsc("sc_M", [1, 1024]).ap()
    sc_W = dsc("sc_W", [64, SEG]).ap()
    sc_mid = dsc("sc_mid", [1, 20000], F32).ap()
    sc_ced = dsc("sc_ced", [64, SPAN]).ap()
    cc_in_t = dsc("cc_in", [4 * 64, DI], F32)
    cc_out_t = dsc("cc_out", [4 * 64, DI], F32, addr_space="Shared")

    with ExitStack() as es:
        E = es.enter_context

        def S(name, shape, dt=F32):
            return E(nc.sbuf_tensor(name, shape, dt))

        wxb = S("wxb", [128, 16 * 129], BF16)
        convbc = S("convbc", [128, 16])
        dcolb = S("dcolb", [128, 16])
        xsq = S("xsq", [128, 1024])
        xnq = S("xnq", [128, 1024])
        qa = S("qa", [128, 1024])
        qb = S("qb", [128, 1024])
        qc = S("qc", [128, 600])
        qo = S("qo", [128, 1024], BF16)
        colb = S("colb", [128, 8])
        rows = S("rows", [1, 2048])
        wrow = S("wrow", [1, 4096], BF16)
        onecol = S("onecol", [128, 1], BF16)
        onecol64 = S("onecol64", [64, 1], BF16)
        onerow64 = S("onerow64", [1, 64])
        ones128r = S("ones128r", [1, 128])
        arow = S("arow", [1, 64])
        maskb = S("maskb", [1, 1024])
        sselb = S("sselb", [64, 4])
        rselb = S("rselb", [64, 4])
        negmb = S("negmb", [1, 512], BF16)
        cedstg = S("cedstg", [64, 64], BF16)
        hsb = S("hsb", [64, 2048], BF16)
        hfb = S("hfb", [64, 2048])
        dtrow = S("dtrow", [1, 512])
        tlrow = S("tlrow", [1, 512])
        rtlrow = S("rtlrow", [1, 512])
        strow = S("strow", [1, 16])
        mrow = S("mrow", [1, 1024], BF16)
        estash = S("estash", [128, 48], BF16)
        ccst = S("ccst", [64, 2048])
        epsb = S("epsb", [128, 1])
        es1 = ExitStack()
        E1 = es1.enter_context

        def S1(name, shape, dt=F32):
            return E1(nc.sbuf_tensor(name, shape, dt))

        winb = S1("winb", [128, 8 * 4096], BF16)
        convdbs = S1("convdbs", [128, 512], BF16)
        xTb = S1("xTb", [128, 8 * 512], BF16)
        xinb = S1("xinb", [128, 16 * 515], BF16)
        zstg = S1("zstg", [128, 512], BF16)
        bb = S1("bb", [64, 512])
        cb = S1("cb", [64, 512])
        dtcm = S1("dtcm", [8, 64])
        tlcm = S1("tlcm", [8, 64])
        t2cm = S1("t2cm", [8, 64])
        rtlcm = S1("rtlcm", [8, 64])
        dlcm = S1("dlcm", [8, 2048])
        dlrow = S1("dlrow", [1, 2048])
        einb = S1("einb", [64, 512])
        eodb = S1("eodb", [64, 512])
        bt = S1("bt", [64, 512], BF16)
        ce = S1("ce", [64, 512], BF16)
        cx = S1("cx", [64, 512], BF16)
        bdt = S1("bdt", [64, 512], BF16)
        cbb = S1("cbb", [64, 512], BF16)
        wfm = S1("wfm", [64, 512], BF16)
        lamt = S1("lamt", [64, 16])
        lamp = S1("lamp", [64, 8])
        rdc = S1("rdc", [64, 1])
        ebuf = S1("ebuf", [64, 1024], BF16)
        qbufb = S1("qbufb", [64, 1024], BF16)
        mpair = S1("mpair", [128, 64], BF16)
        modb = S1("modb", [32, 32], BF16)
        xctm = S1("xctm", [128, 2048], BF16)
        wtm = S1("wtm", [128, 4 * 64], BF16)
        ytm = S1("ytm", [64, 512], BF16)
        hp = E(nc.psum_tensor("hp", [64, 2048], F32))
        pp = E(nc.psum_tensor("pp", [128, 1024], F32))
        yp = E(nc.psum_tensor("yp", [128, 512], F32))
        mp = E(nc.psum_tensor("mp", [128, 512], F32))

        for s in SEMS:
            P.sem[s] = E(nc.semaphore(s))
        blk = E(nc.Block())

        cc_in = cc_in_t.ap()
        cc_out = cc_out_t.ap()

        for k in range(8):
            dma2(winb[:, k * 4096:(k + 1) * 4096], winT[128 * k:128 * (k + 1), :],
                 deps=())
        dma2(wxb.ap().rearrange("p (c n) -> p c n", n=129),
             wxT.rearrange("(c p) n -> p c n", p=128), deps=())
        dma1(convbc[:, :], convb, deps=())
        dma1(dcolb[:, :], dcol, deps=())
        dma1(maskb[:, :], maskrow, deps=())
        dma1(sselb[:, :], slotsel, deps=())
        dma1(rselb[:, :], recvsel, deps=())
        dma1(arow[:, :], alog, deps=())

        ve(lambda e: e.memset(onecol[:], 1.0), drain=False)
        ve(lambda e: e.memset(onecol64[:], 1.0), drain=False)
        ve(lambda e: e.memset(onerow64[:], 1.0), drain=False)
        ve(lambda e: e.memset(ones128r[:], 1.0), drain=False)
        ve(lambda e: e.memset(mpair.ap(), 0.0), drain=False)
        ve(lambda e: e.memset(rdc[:], 1.0), drain=False)
        ve(lambda e: e.memset(hsb.ap(), 0.0), drain=False)
        ve(lambda e: e.memset(hfb.ap(), 0.0), drain=False)
        ve(lambda e: e.memset(epsb[:], 1e-5), drain=False)
        se(lambda e: e.activation(arow[:], arow[:], AF.Exp), deps=("d1",))
        ve(lambda e: e.tensor_scalar(arow[:], arow[:], -1.0, None, op0=ALU.mult),
           deps=("se",))

        for n in range(8):
            def mk(n=n):
                def f(e):
                    ins = None
                    for k in range(8):
                        ins = e.matmul(pp[0:1, 0:512], onecol[:, :],
                                       winb[:, k * 4096 + n * 512:k * 4096 + (n + 1) * 512],
                                       start=(k == 0), stop=(k == 7))
                    return ins
                return f
            pe(mk(), deps=("d2", "ve"))
            ve(lambda e, n=n: e.tensor_copy(wrow[0:1, n * 512:(n + 1) * 512],
                                            pp[0:1, 0:512]), deps=("pe",))
            pe(lambda e: e.nop(), deps=("ve",))

        def quant_tile(width):
            nblk = width // 16
            se(lambda e: e.activation(qa[:, 0:width], xnq[:, 0:width], AF.Abs),
               deps=("ve", "d1"), drain=False)
            se(lambda e: e.sign(qb[:, 0:width], xnq[:, 0:width]))
            ve(lambda e: e.reduce_max(
                qc[:, 0:nblk], qa[:, 0:width].rearrange("p (g b) -> p g b", b=16),
                axis=AX.X), deps=("se",))
            ve(lambda e: e.tensor_scalar(qc[:, nblk:2 * nblk], qc[:, 0:nblk],
                                         1.0 / 6.0, 1e-12, op0=ALU.mult, op1=ALU.max))
            ve(lambda e: e.reciprocal(qc[:, 2 * nblk:3 * nblk], qc[:, nblk:2 * nblk]))
            ve(lambda e: e.tensor_scalar(qc[:, 3 * nblk:4 * nblk], qc[:, nblk:2 * nblk],
                                         0.5, None, op0=ALU.mult))
            ve(lambda e: e.tensor_mul(
                qa[:, 0:width].rearrange("p (g b) -> p g b", b=16),
                qa[:, 0:width].rearrange("p (g b) -> p g b", b=16),
                qc[:, 2 * nblk:3 * nblk].unsqueeze(2).broadcast_to([128, nblk, 16])))
            ve(lambda e: e.tensor_scalar(xnq[:, 0:width], qa[:, 0:width],
                                         MIDS[0], None, op0=ALU.is_gt))
            for j, w in ((1, None), (2, None), (3, None), (4, 2.0), (5, 2.0), (6, 4.0)):
                if w is None:
                    ve(lambda e, j=j: e.tensor_scalar(xsq[:, 0:width], qa[:, 0:width],
                                                      MIDS[j], None, op0=ALU.is_gt))
                else:
                    ve(lambda e, j=j, w=w: e.tensor_scalar(
                        xsq[:, 0:width], qa[:, 0:width], MIDS[j], w,
                        op0=ALU.is_gt, op1=ALU.mult))
                ve(lambda e: e.tensor_add(xnq[:, 0:width], xnq[:, 0:width],
                                          xsq[:, 0:width]))
            ve(lambda e: e.tensor_mul(xnq[:, 0:width], xnq[:, 0:width], qb[:, 0:width]))
            ve(lambda e: e.tensor_mul(
                qo[:, 0:width].rearrange("p (g b) -> p g b", b=16),
                xnq[:, 0:width].rearrange("p (g b) -> p g b", b=16),
                qc[:, 3 * nblk:4 * nblk].unsqueeze(2).broadcast_to([128, nblk, 16])))

        for wsrc, wdst in ((wg, sc_wgq), (wu, sc_wuq)):
            for t in range(22):
                dma1(xnq[:, 0:1024], wsrc[128 * t:128 * (t + 1), :])
                quant_tile(1024)
                dma1(wdst[128 * t:128 * (t + 1), :], qo[:, 0:1024])
        for t in range(8):
            for p0, w in ((0, 1024), (1024, 1024), (2048, 768)):
                dma1(xnq[:, 0:w], wd[128 * t:128 * (t + 1), p0:p0 + w])
                quant_tile(w)
                dma1(sc_wdq[128 * t:128 * (t + 1), p0:p0 + w], qo[:, 0:w])

        def front_proj(c0, T, dst0, zc0, do_z, nf):
            for k in range(8):
                dma2(xTb[:, k * 512:k * 512 + T], xT[128 * k:128 * (k + 1), c0:c0 + T])

            def mms(e):
                ins = None
                for k in range(8):
                    ins = e.matmul(pp[0:1, 512:512 + T], onecol[:, :],
                                   xTb[:, k * 512:k * 512 + T],
                                   start=(k == 0), stop=(k == 7))
                return ins
            pe(mms, deps=("d2",))
            for k in range(8):
                se(lambda e, k=k: e.square(xsq[:, 0:T], xTb[:, k * 512:k * 512 + T]),
                   deps=("d2", "pe"))
                ve(lambda e: e.tensor_copy(qo[:, 0:T], xsq[:, 0:T]), deps=("se",))

                def mks(e, k=k):
                    return e.matmul(pp[32:33, 512:512 + T], onecol[:, :], qo[:, 0:T],
                                    start=(k == 0), stop=(k == 7))
                pe(mks, deps=("ve",))
            ve(lambda e: e.tensor_scalar(rows[0:1, 0:T], pp[0:1, 512:512 + T],
                                         1.0 / DM, None, op0=ALU.mult), deps=("pe",))
            ve(lambda e: e.tensor_scalar(rows[0:1, 512:512 + T], rows[0:1, 0:T],
                                         -1.0, None, op0=ALU.mult))
            ve(lambda e: e.tensor_mul(rows[0:1, 1024:1024 + T], rows[0:1, 0:T],
                                      rows[0:1, 0:T]))
            ve(lambda e: e.tensor_scalar(xsq[0:1, 0:T], pp[32:33, 512:512 + T],
                                         1.0 / DM, None, op0=ALU.mult))
            ve(lambda e: e.tensor_sub(rows[0:1, 1536:1536 + T], xsq[0:1, 0:T],
                                      rows[0:1, 1024:1024 + T]))
            se(lambda e: e.activation(rows[0:1, 1024:1024 + T], rows[0:1, 1536:1536 + T],
                                      AF.Sqrt, bias=epsb[0:1, 0:1], scale=1.0),
               deps=("ve",))
            ve(lambda e: e.reciprocal(rows[0:1, 1536:1536 + T], rows[0:1, 1024:1024 + T]),
               deps=("se",))
            ve(lambda e: e.tensor_copy(negmb[0:1, 0:T], rows[0:1, 512:512 + T]))
            pe(lambda e: e.matmul(pp[0:128, 0:T], ones128r[0:1, :],
                                  rows[0:1, 1536:1536 + T], start=True, stop=True),
               deps=("ve",))
            ve(lambda e: e.tensor_copy(qb[:, 0:T], pp[:, 0:T]), deps=("pe",))
            for f in range(nf):
                def mkf(f=f):
                    def g(e):
                        ins = None
                        for k in range(8):
                            ins = e.matmul(
                                mp[:, 0:T],
                                winb[:, k * 4096 + f * 128:k * 4096 + (f + 1) * 128],
                                xTb[:, k * 512:k * 512 + T],
                                start=(k == 0), stop=False)
                        return e.matmul(mp[:, 0:T], wrow[0:1, f * 128:(f + 1) * 128],
                                        negmb[0:1, 0:T], start=False, stop=True)
                    return g
                pe(mkf(), deps=("ve",))
                if f < 16:
                    ve(lambda e, f=f: e.tensor_mul(
                        xinb[:, f * 515 + dst0:f * 515 + dst0 + T],
                        mp[:, 0:T], qb[:, 0:T]), deps=("pe",))
                else:
                    ve(lambda e: e.tensor_mul(zstg[:, 0:T], mp[:, 0:T], qb[:, 0:T]),
                       deps=("pe",))
                    dma1(sc_z[128 * (f - 16):128 * (f - 15), zc0:zc0 + T],
                         zstg[:, 0:T])
                pe(lambda e: e.nop(), deps=("ve",))

        def front_seg(s):
            S0 = s * SEG
            if s == 0:
                front_proj(0, 3, 0, 0, do_z=False, nf=16)
            else:
                ve(lambda e: e.tensor_copy(
                    xinb.ap().rearrange("p (f c) -> p f c", c=515)[:, :, 0:3],
                    estash.ap().rearrange("p (f c) -> p f c", c=3)))
            front_proj(3 + S0, SEG, 3, S0, do_z=True, nf=32)
            for j in range(4):
                dma1(qa[:, 0:1024], xtm[S0 + 128 * j:S0 + 128 * (j + 1), :],
                     deps=("ve", "se", "pe", "d1"))
                xv = qa[:, 0:1024]
                ve(lambda e, xv=xv, j=j: e.reduce_sum(colb[:, 0:1], xv, axis=AX.X),
                   deps=("d1",))
                se(lambda e, xv=xv: e.square(xsq[:, 0:1024], xv))
                ve(lambda e: e.reduce_sum(colb[:, 1:2], xsq[:, 0:1024], axis=AX.X),
                   deps=("se",))
                ve(lambda e: e.tensor_scalar(colb[:, 2:3], colb[:, 0:1], 1.0 / DM,
                                             None, op0=ALU.mult))
                ve(lambda e: e.tensor_mul(colb[:, 3:4], colb[:, 2:3], colb[:, 2:3]))
                ve(lambda e: e.tensor_scalar(colb[:, 4:5], colb[:, 1:2], 1.0 / DM,
                                             None, op0=ALU.mult))
                ve(lambda e: e.tensor_sub(colb[:, 5:6], colb[:, 4:5], colb[:, 3:4]))
                se(lambda e: e.activation(colb[:, 6:7], colb[:, 5:6], AF.Sqrt,
                                          bias=epsb[:, 0:1], scale=1.0), deps=("ve",))
                ve(lambda e: e.reciprocal(colb[:, 7:8], colb[:, 6:7]), deps=("se",))
                ve(lambda e, xv=xv: e.tensor_scalar(
                    xnq[:, 0:1024], xv, colb[:, 2:3], colb[:, 7:8],
                    op0=ALU.subtract, op1=ALU.mult))
                quant_tile(1024)
                dma1(sc_xq[S0 + 128 * j:S0 + 128 * (j + 1), :], qo[:, 0:1024])
            ve(lambda e: e.tensor_copy(
                estash.ap().rearrange("p (f c) -> p f c", c=3),
                xinb.ap().rearrange("p (f c) -> p f c", c=515)[:, :, 512:515]))
            for f in range(16):
                dma2(convdbs[:, :], convd[:, f * 512:(f + 1) * 512], deps=("pe",))
                def mkc(f=f):
                    def g(e):
                        ins = None
                        for k in range(4):
                            ins = e.matmul(
                                mp[:, 0:512],
                                convdbs[:, k * 128:(k + 1) * 128],
                                xinb[:, f * 515 + k:f * 515 + k + 512],
                                start=(k == 0), stop=(k == 3))
                        return ins
                    return g
                pe(mkc(), deps=("ve", "d2"))
                se(lambda e, f=f: e.activation(xinb[:, f * 515 + 3:f * 515 + 515],
                                               mp[:, 0:512], AF.Silu,
                                               bias=convbc[:, f:f + 1], scale=1.0),
                   deps=("pe",))
                pe(lambda e: e.nop(), deps=("se",))
                dma1(sc_xc[128 * f:128 * (f + 1), S0:S0 + SEG],
                     xinb[:, f * 515 + 3:f * 515 + 515])
                if s == 0 and f == 0:
                    dma1(dbg[:, 1024:1536], xinb[:, 3:515])
            def mb(e):
                ins = None
                for c in range(16):
                    ins = e.matmul(pp[0:64, 0:512], wxb[:, c * 129:c * 129 + 64],
                                   xinb[:, c * 515 + 3:c * 515 + 515],
                                   start=(c == 0), stop=(c == 15))
                return ins
            pe(mb, deps=("se",))
            ve(lambda e: e.tensor_copy(bb[:, :], pp[0:64, 0:512]), deps=("pe",))

            def mc(e):
                ins = None
                for c in range(16):
                    ins = e.matmul(pp[64:128, 0:512], wxb[:, c * 129 + 64:c * 129 + 128],
                                   xinb[:, c * 515 + 3:c * 515 + 515],
                                   start=(c == 0), stop=(c == 15))
                return ins
            pe(mc, deps=("se",))
            ve(lambda e: e.tensor_copy(cb[:, :], pp[64:128, 0:512]), deps=("pe",))

            def md(e):
                ins = None
                for c in range(16):
                    ins = e.matmul(pp[0:1, 512:1024], wxb[:, c * 129 + 128:c * 129 + 129],
                                   xinb[:, c * 515 + 3:c * 515 + 515],
                                   start=(c == 0), stop=(c == 15))
                return ins
            pe(md, deps=("ve",))
            se(lambda e: e.activation(rows[0:1, 0:512], pp[0:1, 512:1024], AF.Abs),
               deps=("pe",))
            se(lambda e: e.activation(rows[0:1, 512:1024], rows[0:1, 0:512], AF.Exp,
                                      bias=0.0, scale=-1.0))
            ve(lambda e: e.tensor_scalar(rows[0:1, 512:1024], rows[0:1, 512:1024],
                                         1.0, None, op0=ALU.add), deps=("se",))
            se(lambda e: e.activation(rows[0:1, 1024:1536], rows[0:1, 512:1024], AF.Ln),
               deps=("ve",))
            ve(lambda e: e.tensor_scalar(dtrow[0:1, :], pp[0:1, 512:1024], 0.0, None,
                                         op0=ALU.max), deps=("se",))
            ve(lambda e: e.tensor_add(dtrow[0:1, :], dtrow[0:1, :],
                                      rows[0:1, 1024:1536]))
            pe(lambda e: e.nop(), deps=("ve",))

        def mid_seg(s):
            S0 = s * SEG
            dma1(sc_mid[0:1, 0:512], dtrow[0:1, :])
            dma1(dtcm[:, :], sc_mid[0:1, 0:512].rearrange("o (c t) -> (o c) t", t=64),
                 deps=("d1",))
            ve(lambda e: e.memset(t2cm[:, :], 0.0), deps=("d1",), drain=False)
            P.step("vector", lambda e: e.drain())
            ve(lambda e: e.tensor_tensor_scan(tlcm[:, :], dtcm[:, :], t2cm[:, :], 0.0,
                                              op0=ALU.add, op1=ALU.add), deps=("d1",))
            ve(lambda e: e.tensor_copy(t2cm[:, :], tlcm[:, :]))
            ve(lambda e: e.tensor_scalar(tlcm[:, 32:64], t2cm[:, 32:64],
                                         t2cm[:, 31:32], None, op0=ALU.subtract))
            ve(lambda e: e.tensor_scalar(rtlcm[:, 0:32], tlcm[:, 0:32], -1.0,
                                         tlcm[:, 31:32], op0=ALU.mult, op1=ALU.add))
            ve(lambda e: e.tensor_scalar(rtlcm[:, 32:64], tlcm[:, 32:64], -1.0,
                                         tlcm[:, 63:64], op0=ALU.mult, op1=ALU.add))
            ve(lambda e: e.tensor_sub(
                dlcm.ap().rearrange("p (ti a t) -> p ti a t", a=32, t=32),
                tlcm.ap().rearrange("p (ti t) -> p ti t", t=32).unsqueeze(2)
                .broadcast_to([8, 2, 32, 32]),
                tlcm.ap().rearrange("p (ti a) -> p ti a", a=32).unsqueeze(3)
                .broadcast_to([8, 2, 32, 32])))
            dma1(sc_mid[0:1, 1024:1536].rearrange("o (p t) -> (o p) t", t=64),
                 tlcm[:, :])
            dma1(sc_mid[0:1, 1536:2048].rearrange("o (p t) -> (o p) t", t=64),
                 rtlcm[:, :])
            dma1(sc_mid[0:1, 2048:18432].rearrange("o (p t) -> (o p) t", t=2048),
                 dlcm[:, :])
            dma1(sc_mid[0:1, 18432:18448].rearrange("o (p a) -> (o p) a", a=2).unsqueeze(2),
                 tlcm.ap().rearrange("p (a t) -> p a t", t=32)[:, :, 31:32],
                 allow_slow_non_contiguous=True)
            dma1(tlrow[0:1, :], sc_mid[0:1, 1024:1536], deps=("d1",))
            dma1(rtlrow[0:1, :], sc_mid[0:1, 1536:2048], deps=("d1",))
            dma1(strow[0:1, :], sc_mid[0:1, 18432:18448], deps=("d1",))
            pe(lambda e: e.matmul(mp[0:64, 0:512], arow[:, :], tlrow[0:1, :],
                                  start=True, stop=True), deps=("d1",))
            se(lambda e: e.activation(einb[:, :], mp[0:64, 0:512], AF.Exp),
               deps=("pe",))
            pe(lambda e: e.matmul(mp[0:64, 0:512], arow[:, :], rtlrow[0:1, :],
                                  start=True, stop=True), deps=("se",))
            se(lambda e: e.activation(eodb[:, :], mp[0:64, 0:512], AF.Exp),
               deps=("pe",))
            pe(lambda e: e.matmul(mp[0:64, 0:512], onerow64[:, :], dtrow[0:1, :],
                                  start=True, stop=True), deps=("se",))
            ve(lambda e: e.tensor_mul(eodb[:, :], eodb[:, :], mp[0:64, 0:512]),
               deps=("pe",))
            ve(lambda e: e.tensor_mul(bdt[:, :], bb[:, :], mp[0:64, 0:512]))
            ve(lambda e: e.tensor_copy(cbb[:, :], cb[:, :]))
            pe(lambda e: e.matmul(mp[0:64, 0:16], arow[:, :], strow[0:1, :],
                                  start=True, stop=True), deps=("ve",))
            se(lambda e: e.activation(lamt[:, :], mp[0:64, 0:16], AF.Exp), deps=("pe",))
            ve(lambda e: e.tensor_mul(lamp[:, :], lamt.ap()[:, 0:16:2],
                                      lamt.ap()[:, 1:16:2]), deps=("se",))
            ve(lambda e: e.tensor_mul(bt[:, :], bb[:, :], eodb[:, :]))
            ve(lambda e: e.tensor_mul(ce[:, :], cb[:, :], einb[:, :]))
            for p in range(8):
                ve(lambda e, p=p: e.tensor_copy(cx[:, 64 * p:64 * p + 32],
                                                ce[:, 64 * p:64 * p + 32]), drain=False)
                ve(lambda e, p=p: e.tensor_scalar(
                    cx[:, 64 * p + 32:64 * p + 64], ce[:, 64 * p + 32:64 * p + 64],
                    lamt[:, 2 * p:2 * p + 1], None, op0=ALU.mult))
                ve(lambda e, p=p: e.tensor_scalar(
                    cedstg[:, :], cx[:, 64 * p:64 * (p + 1)],
                    rdc[:, 0:1], None, op0=ALU.mult))
                dma1(sc_ced[:, S0 + 64 * p:S0 + 64 * (p + 1)], cedstg[:, :])
                ve(lambda e, p=p: e.tensor_mul(rdc[:, 0:1], rdc[:, 0:1],
                                               lamp[:, p:p + 1]))
                ve(lambda e, p=p: e.tensor_scalar(
                    wfm[:, 64 * p:64 * p + 32], bt[:, 64 * p:64 * p + 32],
                    lamt[:, 2 * p + 1:2 * p + 2], None, op0=ALU.mult), drain=False)
                ve(lambda e, p=p: e.tensor_copy(wfm[:, 64 * p + 32:64 * p + 64],
                                                bt[:, 64 * p + 32:64 * p + 64]))
            dma1(sc_W[:, :], wfm[:, :])
            for g in range(4):
                dma1t(wtm[:, g * 64:(g + 1) * 64],
                      sc_W[:, 128 * g:128 * (g + 1)], deps=("d1",))
            for p in range(8):
                if p % 2 == 0:
                    g = p // 2
                    dma1t(xctm[:, 0:2048],
                          sc_xc[:, S0 + 128 * g:S0 + 128 * (g + 1)],
                          deps=("d1", "pe"))
                chunk_64(s, p)

        def chunk_64(s, p):
            S0 = s * SEG
            glob = s * 8 + p
            g2 = 0
            roff = 64 * (p % 2)
            dma1(dlrow[0:1, :], sc_mid[0:1, 2048 + p * 2048:2048 + (p + 1) * 2048],
                 deps=("d1", "pe"))
            for ti in range(2):
                def mkp(ti=ti, p=p):
                    def g(e):
                        base = ti * 1024
                        e.matmul(pp[0:64, 0:512], arow[:, :],
                                 dlrow[0:1, base:base + 512], start=True, stop=False)
                        e.matmul(pp[0:64, 0:512], onerow64[:, :], maskb[0:1, 0:512],
                                 start=False, stop=True)
                        e.matmul(pp[0:64, 512:1024], arow[:, :],
                                 dlrow[0:1, base + 512:base + 1024], start=True,
                                 stop=False)
                        return e.matmul(pp[0:64, 512:1024], onerow64[:, :],
                                        maskb[0:1, 512:1024], start=False, stop=True)
                    return g
                pe(mkp(), deps=("ve", "d1", "se"))
                se(lambda e: e.activation(ebuf[:, :], pp[0:64, 0:1024], AF.Exp),
                   deps=("pe",))
                tb = 64 * p + 32 * ti
                ve(lambda e, tb=tb: e.tensor_mul(
                    ebuf.ap().rearrange("p (a t) -> p a t", t=32),
                    ebuf.ap().rearrange("p (a t) -> p a t", t=32),
                    bdt[:, tb:tb + 32].unsqueeze(2).broadcast_to([64, 32, 32])),
                   deps=("se",))
                ve(lambda e, tb=tb: e.tensor_mul(
                    qbufb.ap().rearrange("p (a t) -> p a t", t=32),
                    ebuf.ap().rearrange("p (a t) -> p a t", t=32),
                    cbb[:, tb:tb + 32].unsqueeze(1).broadcast_to([64, 32, 32])))
                def mkr(e):
                    e.matmul(pp[0:1, 0:512], onecol64[:, :], qbufb[:, 0:512],
                             start=True, stop=True)
                    return e.matmul(pp[0:1, 512:1024], onecol64[:, :],
                                    qbufb[:, 512:1024], start=True, stop=True)
                pe(mkr, deps=("ve",))
                se(lambda e: e.activation(mrow[0:1, :], pp[0:1, 0:1024], AF.Identity)
                   if False else e.copy(mrow[0:1, :], pp[0:1, 0:1024]), deps=("pe",))
                dma1(sc_M[0:1, :], mrow[0:1, :])
                dma1(mpair[roff + 32 * ti:roff + 32 * ti + 32, 32 * ti:32 * ti + 32],
                     sc_M[0:1, :].rearrange("o (a t) -> (o a) t", t=32), deps=("d1",))
            pe(lambda e, p=p: e.matmul(mp[0:32, 0:32], bt[:, 64 * p:64 * p + 32],
                                       ce[:, 64 * p + 32:64 * p + 64],
                                       start=True, stop=True), deps=("ve", "d1"))
            se(lambda e: e.copy(modb[:, :], mp[0:32, 0:32]), deps=("pe",))
            dma2(mpair[roff:roff + 32, 32:64], modb[:, :], deps=("se",))
            for dt_ in range(4):
                def mk2(dt_=dt_, p=p, g2=g2, roff=roff, glob=glob):
                    def g(e):
                        ins = e.matmul(yp[0:64, 0:512], mpair[roff:roff + 64, :],
                                       xctm[roff:roff + 64,
                                            g2 * 2048 + dt_ * 512:g2 * 2048 + (dt_ + 1) * 512],
                                       start=True, stop=(glob == 0))
                        if glob > 0:
                            e.matmul(yp[0:32, 0:512], ce[:, 64 * p:64 * p + 32],
                                     hsb[:, dt_ * 512:(dt_ + 1) * 512],
                                     start=False, stop=False)
                            ins = e.matmul(yp[32:64, 0:512],
                                           cx[:, 64 * p + 32:64 * p + 64],
                                           hsb[:, dt_ * 512:(dt_ + 1) * 512],
                                           start=False, stop=True)
                        return ins
                    return g
                pe(mk2(), deps=("d2", "ve", "d1"))
                ve(lambda e: e.tensor_copy(ytm[:, 0:512], yp[0:64, 0:512]),
                   deps=("pe",))
                pe(lambda e: e.nop(), deps=("ve",))
                dma1(sc_y[S0 + 64 * p:S0 + 64 * (p + 1), dt_ * 512:(dt_ + 1) * 512],
                     ytm[:, 0:512])
            def mks(p=p, g2=g2, roff=roff):
                def g(e):
                    ins = None
                    for dt_ in range(4):
                        ins = e.matmul(hp[:, dt_ * 512:(dt_ + 1) * 512],
                                       wtm[roff:roff + 64,
                                           (p // 2) * 64:(p // 2) * 64 + 64],
                                       xctm[roff:roff + 64,
                                            g2 * 2048 + dt_ * 512:g2 * 2048 + (dt_ + 1) * 512],
                                       start=True, stop=True)
                    return ins
                return g
            pe(mks(), deps=("ve", "d1"))
            if glob > 0:
                ve(lambda e, p=p: e.scalar_tensor_tensor(
                    hfb[:, :], hfb[:, :], lamp[:, p:p + 1], hp[:, :],
                    op0=ALU.mult, op1=ALU.add), deps=("pe",))
            else:
                ve(lambda e: e.tensor_copy(hfb[:, :], hp[:, :]), deps=("pe",))
            ve(lambda e: e.tensor_copy(hsb[:, :], hfb[:, :]))
            pe(lambda e: e.nop(), deps=("ve",))

        for s in range(NSEG):
            front_seg(s)
            mid_seg(s)

        for j in range(4):
            ve(lambda e, j=j: e.tensor_scalar(ccst[:, :], hfb[:, :],
                                              sselb[:, j:j + 1], None, op0=ALU.mult))
            dma1(cc_in[64 * j:64 * (j + 1), :], ccst[:, :])
        P.step("gpsimd",
               lambda e: e.collective_compute(
                   "AllReduce", ALU.add, replica_groups=[list(range(8))],
                   ins=[cc_in.opt()], outs=[cc_out.opt()]),
               deps=("d1",), inc="cc", n=1)

        barrier()
        es1.close()
        es2 = ExitStack()
        E2 = es2.enter_context

        def S2(name, shape, dt=F32):
            return E2(nc.sbuf_tensor(name, shape, dt))

        xqTs = S2("xqTs", [128, 8 * 4096], BF16)
        wstgA = S2("wstgA", [128, 8 * 512], BF16)
        wstgB = S2("wstgB", [128, 8 * 512], BF16)
        gsilA = S2("gsilA", [128, 512])

        for k in range(8):
            dma1t(xqTs[:, k * 4096:(k + 1) * 4096],
                  sc_xq[:, 128 * k:128 * (k + 1)], deps=("d1",))
        NT = SPAN // 128
        for nchunk in range(6):
            n0 = nchunk * 512
            nw = min(512, IF - n0)
            for k in range(8):
                dma1t(wstgA[:, k * 512:k * 512 + nw],
                      sc_wgq[n0:n0 + nw, 128 * k:128 * (k + 1)], deps=("d1", "pe"))
                dma1t(wstgB[:, k * 512:k * 512 + nw],
                      sc_wuq[n0:n0 + nw, 128 * k:128 * (k + 1)], deps=("d1", "pe"))
            for t in range(NT):
                def mg(t=t, nw=nw):
                    def g(e):
                        ins = None
                        for k in range(8):
                            ins = e.matmul(yp[:, 0:nw],
                                           xqTs[:, k * 4096 + t * 128:k * 4096 + (t + 1) * 128],
                                           wstgA[:, k * 512:k * 512 + nw],
                                           start=(k == 0), stop=(k == 7))
                        return ins
                    return g
                pe(mg(), deps=("d1",))
                se(lambda e, nw=nw: e.activation(gsilA[:, 0:nw], yp[:, 0:nw], AF.Silu),
                   deps=("pe",))

                def mu(t=t, nw=nw):
                    def g(e):
                        ins = None
                        for k in range(8):
                            ins = e.matmul(mp[:, 0:nw],
                                           xqTs[:, k * 4096 + t * 128:k * 4096 + (t + 1) * 128],
                                           wstgB[:, k * 512:k * 512 + nw],
                                           start=(k == 0), stop=(k == 7))
                        return ins
                    return g
                pe(mu(), deps=("se",))
                ve(lambda e, nw=nw: e.tensor_mul(xnq[:, 0:nw], gsilA[:, 0:nw],
                                                 mp[:, 0:nw]), deps=("pe", "se"))
                pe(lambda e: e.nop(), deps=("ve",))
                quant_tile(nw)
                dma1(sc_hq[128 * t:128 * (t + 1), n0:n0 + nw], qo[:, 0:nw])
                if t == 0 and nchunk == 0:
                    dma1(dbg[:, 1536:2048], qo[:, 0:512])
                    dma2(dbg[:, 3072:3584], xnq[:, 0:512])

        barrier()
        es2.close()
        es3 = ExitStack()
        E3 = es3.enter_context

        def S3(name, shape, dt=F32):
            return E3(nc.sbuf_tensor(name, shape, dt))

        woutb = S3("woutb", [128, 16 * 1024], BF16)
        wdqb = S3("wdqb", [128, 22 * 1024], BF16)
        hqTs = S3("hqTs", [128, 22 * 512], BF16)
        ygfm = S3("ygfm", [128, 16 * 512], BF16)
        yfmt = S3("yfmt", [128, 512], BF16)
        xctb = S3("xctb", [128, 512], BF16)
        zstg2 = S3("zstg2", [128, 512], BF16)
        gsilB = S3("gsilB", [128, 512])
        hrecv = S3("hrecv", [64, 2048])
        hrecvb = S3("hrecvb", [64, 2048], BF16)
        outsb = xnq
        xres = qa
        cedt = S3("cedt", [64, 512], BF16)

        ve(lambda e: e.memset(hrecv[:, :], 0.0), deps=("cc",))
        for j in range(4):
            dma1(ccst[:, :], cc_out[64 * j:64 * (j + 1), :], deps=("cc", "ve"))
            ve(lambda e, j=j: e.tensor_scalar(hfb[:, :], ccst[:, :],
                                              rselb[:, j:j + 1], None, op0=ALU.mult),
               deps=("d1",))
            ve(lambda e: e.tensor_add(hrecv[:, :], hrecv[:, :], hfb[:, :]))
        ve(lambda e: e.tensor_copy(hrecvb[:, :], hrecv[:, :]))

        for k in range(16):
            dma2(woutb[:, k * 1024:(k + 1) * 1024], woutT[128 * k:128 * (k + 1), :],
                 deps=("pe", "ve"))
        for i in range(22):
            dma1t(wdqb[:, i * 1024:(i + 1) * 1024],
                  sc_wdq[:, 128 * i:128 * (i + 1)], deps=("d1", "pe"))
        for s in range(NSEG):
            S0 = s * SEG
            dma1(cedt[:, :], sc_ced[:, S0:S0 + SEG], deps=("ve", "pe"))
            for f in range(16):
                dma1t(yfmt[:, :], sc_y[S0:S0 + SEG, 128 * f:128 * (f + 1)],
                      deps=("d1", "ve"))
                dma1(xctb[:, :], sc_xc[128 * f:128 * (f + 1), S0:S0 + SEG])
                dma1(zstg2[:, :], sc_z[128 * f:128 * (f + 1), S0:S0 + SEG])

                pe(lambda e, f=f: e.matmul(mp[:, 0:512],
                                           hrecvb[:, f * 128:(f + 1) * 128],
                                           cedt[:, :],
                                           start=True, stop=True),
                   deps=("ve", "d1"))
                ve(lambda e: e.tensor_copy(qa[:, 0:512], yfmt[:, :]), deps=("d1",))
                ve(lambda e: e.tensor_add(qa[:, 0:512], qa[:, 0:512], mp[:, 0:512]),
                   deps=("pe",))
                pe(lambda e: e.nop(), deps=("ve",))
                ve(lambda e, f=f: e.scalar_tensor_tensor(
                    qb[:, 0:512], xctb[:, :], dcolb[:, f:f + 1], qa[:, 0:512],
                    op0=ALU.mult, op1=ALU.add), deps=("d1",))
                se(lambda e: e.activation(gsilB[:, 0:512], zstg2[:, :], AF.Silu),
                   deps=("d1",))
                ve(lambda e, f=f: e.tensor_mul(ygfm[:, f * 512:(f + 1) * 512],
                                               qb[:, 0:512], gsilB[:, 0:512]),
                   deps=("se",))
            for i in range(22):
                dma1t(hqTs[:, i * 512:(i + 1) * 512],
                      sc_hq[S0:S0 + SEG, 128 * i:128 * (i + 1)], deps=("d1", "pe"))
            for t4 in range(4):
                T0 = S0 + 128 * t4
                dma1(xres[:, :], xtm[T0:T0 + 128, :])
                for nh in range(2):
                    def mo(t4=t4, nh=nh):
                        def g(e):
                            ins = None
                            for k in range(16):
                                ins = e.matmul(
                                    yp[:, 0:512],
                                    ygfm[:, (k // 4) * 2048 + 0:0] if False else
                                    ygfm[:, k * 512 + 0:k * 512 + 0] if False else
                                    ygfm.ap().rearrange("p (f t) -> p f t", t=512)
                                    [:, k, 128 * t4:128 * t4 + 128],
                                    woutb[:, k * 1024 + nh * 512:k * 1024 + (nh + 1) * 512],
                                    start=(k == 0), stop=False)
                            for i in range(22):
                                ins = e.matmul(
                                    yp[:, 0:512],
                                    hqTs[:, i * 512 + 128 * t4:i * 512 + 128 * (t4 + 1)],
                                    wdqb[:, i * 1024 + nh * 512:i * 1024 + (nh + 1) * 512],
                                    start=False, stop=(i == 21))
                            return ins
                        return g
                    pe(mo(), deps=("ve", "d1", "d2"))
                    ve(lambda e, nh=nh: e.tensor_add(outsb[:, nh * 512:(nh + 1) * 512],
                                                     xres[:, nh * 512:(nh + 1) * 512],
                                                     yp[:, 0:512]), deps=("pe", "d1"))
                    pe(lambda e: e.nop(), deps=("ve",))
                dma1(out[T0:T0 + 128, :], outsb[:, :])

        P.step("sync", lambda e: e.wait_ge(P.sem["d1"], P.cnt["d1"]))
        P.step("gpsimd", lambda e: e.wait_ge(P.sem["d2"], P.cnt["d2"]))

        @blk.sync
        def _(eng):
            for en, fn in P.steps:
                if en == "sync":
                    fn(eng)

        @blk.vector
        def _(eng):
            for en, fn in P.steps:
                if en == "vector":
                    fn(eng)

        @blk.scalar
        def _(eng):
            for en, fn in P.steps:
                if en == "scalar":
                    fn(eng)

        @blk.tensor
        def _(eng):
            for en, fn in P.steps:
                if en == "tensor":
                    fn(eng)

        @blk.gpsimd
        def _(eng):
            for en, fn in P.steps:
                if en == "gpsimd":
                    fn(eng)

        es3.close()

    return nc


def stage_inputs(inputs, c):
    b, h = c // 2, c % 2
    x = np.asarray(inputs["L0_attn_x"])
    f32 = np.float32
    xb = np.ascontiguousarray(x[b])
    s0 = h * SPAN
    xTfull = np.ascontiguousarray(xb.T)
    halo = np.zeros((DM, 3), f32) if h == 0 else xTfull[:, s0 - 3:s0]
    xTs = np.concatenate([halo, xTfull[:, s0:s0 + SPAN]], axis=1)
    xtm_s = np.ascontiguousarray(xb[s0:s0 + SPAN])
    cw = np.asarray(inputs["L0_attn_conv_w"]).reshape(DI, 4)
    cdd = np.zeros((4, 16, 128, 128), f32)
    for k in range(4):
        for f in range(16):
            np.fill_diagonal(cdd[k, f], cw[128 * f:128 * (f + 1), k])
    convd_s = np.ascontiguousarray(cdd.transpose(2, 1, 0, 3).reshape(128, 64 * 128))
    convb_s = np.ascontiguousarray(
        np.asarray(inputs["L0_attn_conv_b"]).reshape(16, 128).T)
    dcol_s = np.ascontiguousarray(np.asarray(inputs["L0_attn_D"]).reshape(16, 128).T)
    mr = np.zeros((1, 1024), f32)
    for a in range(32):
        for t in range(32):
            if t < a:
                mr[0, 32 * a + t] = -30000.0
    ss = np.zeros((64, 4), f32)
    rs = np.zeros((64, 4), f32)
    if h == 0:
        ss[:, b] = 1.0
    else:
        rs[:, b] = 1.0
    return {
        "xT": np.ascontiguousarray(xTs, f32),
        "xtm": np.ascontiguousarray(xtm_s, f32),
        "winT": np.ascontiguousarray(np.asarray(inputs["L0_attn_W_in"]).T, f32),
        "wxT": np.ascontiguousarray(np.asarray(inputs["L0_attn_W_x"]).T, f32),
        "convd": convd_s.astype(f32),
        "convb": convb_s.astype(f32),
        "alog": np.asarray(inputs["L0_attn_A_log"], f32).reshape(1, DS),
        "dcol": dcol_s.astype(f32),
        "woutT": np.ascontiguousarray(np.asarray(inputs["L0_attn_W_out"]).T, f32),
        "wg": np.asarray(inputs["L0_ffn_w_gate"], f32),
        "wu": np.asarray(inputs["L0_ffn_w_up"], f32),
        "wd": np.asarray(inputs["L0_ffn_w_down"], f32),
        "maskrow": mr,
        "slotsel": ss,
        "recvsel": rs,
    }


def kernel(**inputs):
    nc = build_program()
    in_maps = [stage_inputs(inputs, c) for c in range(8)]
    res = run_bass_kernel_spmd(nc, in_maps, core_ids=list(range(8)))
    outp = np.zeros((B, L, DM), np.float32)
    for c in range(8):
        b, h = c // 2, c % 2
        outp[b, h * SPAN:(h + 1) * SPAN, :] = np.asarray(res.results[c]["out"])
    return outp


# version 04
# speedup vs baseline: 1.0292x; 1.0292x over previous
import numpy as np
from contextlib import ExitStack

import concourse.bass as bass
import concourse.mybir as mybir
from concourse.bass_utils import run_bass_kernel_spmd

F32 = mybir.dt.float32
BF16 = mybir.dt.bfloat16
AF = mybir.ActivationFunctionType
ALU = mybir.AluOpType
AX = mybir.AxisListType

B, L, DM = 4, 8192, 1024
DI, DS = 2048, 64
IF = 2816
SPAN, SEG = 4096, 512
NSEG = SPAN // SEG
MIDS = [0.25, 0.75, 1.25, 1.75, 2.5, 3.5, 5.0]

ENGS = ["sync", "vector", "scalar", "tensor", "gpsimd"]
SEMS = ["d1", "d2", "ve", "se", "pe", "cc", "st"]


class Plan:
    def __init__(self):
        self.steps = []
        self.cnt = {s: 0 for s in SEMS}
        self.seen = {e: {s: 0 for s in SEMS} for e in ENGS}
        self.sem = {}

    def step(self, eng, fn, deps=(), inc=None, n=1):
        if eng in ("vector", "scalar"):
            deps = tuple(deps) + ("d1", "d2")
        waits = []
        for s in deps:
            v = self.cnt[s]
            if v > self.seen[eng][s]:
                waits.append((s, v))
                self.seen[eng][s] = v
        if inc is not None:
            self.cnt[inc] += n
        sem = self.sem

        def run(e, waits=waits, fn=fn, inc=inc, n=n):
            for s, v in waits:
                e.wait_ge(sem[s], v)
            ins = fn(e)
            if inc is not None:
                ins.then_inc(sem[inc], n)
        self.steps.append((eng, run))


P = Plan()


def dma1(o, i, deps=("ve", "se", "pe"), **kw):
    P.step("sync", lambda e, o=o, i=i, kw=kw: e.dma_start(out=o, in_=i, **kw),
           deps=deps, inc="d1", n=16)


def dma1t(o, i, deps=("d1",)):
    P.step("sync", lambda e, o=o, i=i: e.dma_start(out=o, in_=i, transpose=True),
           deps=deps, inc="d1", n=16)


def dma2(o, i, deps=("ve", "se", "pe"), **kw):
    P.step("gpsimd", lambda e, o=o, i=i, kw=kw: e.dma_start(out=o, in_=i, **kw),
           deps=deps, inc="d2", n=16)


def ve(fn, deps=(), drain=True):
    P.step("vector", fn, deps=deps, inc="ve")
    if drain:
        P.step("vector", lambda e: e.drain())


def se(fn, deps=(), drain=True):
    P.step("scalar", fn, deps=deps, inc="se")
    if drain:
        P.step("scalar", lambda e: e.drain())


def pe(fn, deps=()):
    P.step("tensor", fn, deps=deps, inc="pe")


def barrier():
    allsems = ("d1", "d2", "ve", "se", "pe", "cc")
    for eng in ENGS:
        P.step(eng, lambda e: e.nop(), deps=allsems)


def build_program():
    global P
    P = Plan()
    nc = bass.Bass("TRN2", target_bir_lowering=False)

    def din(name, shape):
        return nc.dram_tensor(name, shape, F32, kind="ExternalInput").ap()

    xT = din("xT", [DM, SPAN + 3])
    xtm = din("xtm", [SPAN, DM])
    winT = din("winT", [DM, 2 * DI])
    wxT = din("wxT", [DI, 129])
    convd = din("convd", [128, 64 * 128])
    convb = din("convb", [128, 16])
    alog = din("alog", [1, DS])
    dcol = din("dcol", [128, 16])
    woutT = din("woutT", [DI, DM])
    wg = din("wg", [IF, DM])
    wu = din("wu", [IF, DM])
    wd = din("wd", [DM, IF])
    maskrow = din("maskrow", [1, 1024])
    slotsel = din("slotsel", [64, 4])
    recvsel = din("recvsel", [64, 4])
    out = nc.dram_tensor("out", [SPAN, DM], F32, kind="ExternalOutput").ap()
    dbg = nc.dram_tensor("dbg", [128, 4096], BF16, kind="ExternalOutput").ap()

    def dsc(name, shape, dt=BF16, **kw):
        return nc.dram_tensor(name, shape, dt, kind="Internal", **kw)

    sc_xc = dsc("sc_xc", [DI, SPAN]).ap()
    sc_z = dsc("sc_z", [DI, SPAN]).ap()
    sc_y = dsc("sc_y", [SPAN, DI]).ap()
    sc_xq = dsc("sc_xq", [SPAN, DM]).ap()
    sc_hq = dsc("sc_hq", [SPAN, IF]).ap()
    sc_wgq = dsc("sc_wgq", [IF, DM]).ap()
    sc_wuq = dsc("sc_wuq", [IF, DM]).ap()
    sc_wdq = dsc("sc_wdq", [DM, IF]).ap()
    sc_M = dsc("sc_M", [1, 1024]).ap()
    sc_W = dsc("sc_W", [64, SEG]).ap()
    sc_mid = dsc("sc_mid", [1, 20000], F32).ap()
    sc_ced = dsc("sc_ced", [64, SPAN]).ap()
    cc_in_t = dsc("cc_in", [4 * 64, DI], F32)
    cc_out_t = dsc("cc_out", [4 * 64, DI], F32, addr_space="Shared")

    with ExitStack() as es:
        E = es.enter_context

        def S(name, shape, dt=F32):
            return E(nc.sbuf_tensor(name, shape, dt))

        wxb = S("wxb", [128, 16 * 129], BF16)
        convbc = S("convbc", [128, 16])
        dcolb = S("dcolb", [128, 16])
        xsq = S("xsq", [128, 1024])
        xnq = S("xnq", [128, 1024])
        qa = S("qa", [128, 1024])
        qb = S("qb", [128, 1024])
        qc = S("qc", [128, 600])
        qo = S("qo", [128, 1024], BF16)
        colb = S("colb", [128, 8])
        rows = S("rows", [1, 2048])
        wrow = S("wrow", [1, 4096], BF16)
        onecol = S("onecol", [128, 1], BF16)
        onecol64 = S("onecol64", [64, 1], BF16)
        onerow64 = S("onerow64", [1, 64])
        ones128r = S("ones128r", [1, 128])
        arow = S("arow", [1, 64])
        maskb = S("maskb", [1, 1024])
        sselb = S("sselb", [64, 4])
        rselb = S("rselb", [64, 4])
        negmb = S("negmb", [1, 512], BF16)
        cedstg = S("cedstg", [64, 64], BF16)
        hsb = S("hsb", [64, 2048], BF16)
        hfb = S("hfb", [64, 2048])
        dtrow = S("dtrow", [1, 512])
        tlrow = S("tlrow", [1, 512])
        rtlrow = S("rtlrow", [1, 512])
        strow = S("strow", [1, 16])
        mrow = S("mrow", [1, 1024], BF16)
        estash = S("estash", [128, 48], BF16)
        ccst = S("ccst", [64, 2048])
        epsb = S("epsb", [128, 1])
        es1 = ExitStack()
        E1 = es1.enter_context

        def S1(name, shape, dt=F32):
            return E1(nc.sbuf_tensor(name, shape, dt))

        winb = S1("winb", [128, 8 * 4096], BF16)
        convdbs = S1("convdbs", [128, 512], BF16)
        xTb = S1("xTb", [128, 8 * 512], BF16)
        xinb = S1("xinb", [128, 16 * 515], BF16)
        zstg = S1("zstg", [128, 512], BF16)
        bb = S1("bb", [64, 512])
        cb = S1("cb", [64, 512])
        dtcm = S1("dtcm", [8, 64])
        tlcm = S1("tlcm", [8, 64])
        t2cm = S1("t2cm", [8, 64])
        rtlcm = S1("rtlcm", [8, 64])
        dlcm = S1("dlcm", [8, 2048])
        dlrow = S1("dlrow", [1, 2048])
        einb = S1("einb", [64, 512])
        eodb = S1("eodb", [64, 512])
        bt = S1("bt", [64, 512], BF16)
        ce = S1("ce", [64, 512], BF16)
        cx = S1("cx", [64, 512], BF16)
        bdt = S1("bdt", [64, 512], BF16)
        cbb = S1("cbb", [64, 512], BF16)
        wfm = S1("wfm", [64, 512], BF16)
        lamt = S1("lamt", [64, 16])
        lamp = S1("lamp", [64, 8])
        rdc = S1("rdc", [64, 1])
        ebuf = S1("ebuf", [64, 1024], BF16)
        qbufb = S1("qbufb", [64, 1024], BF16)
        mpair = S1("mpair", [128, 64], BF16)
        modb = S1("modb", [32, 32], BF16)
        xctm = S1("xctm", [128, 2048], BF16)
        wtm = S1("wtm", [128, 4 * 64], BF16)
        ytm = S1("ytm", [64, 512], BF16)
        hp = E(nc.psum_tensor("hp", [64, 2048], F32))
        pp = E(nc.psum_tensor("pp", [128, 1024], F32))
        yp = E(nc.psum_tensor("yp", [128, 512], F32))
        mp = E(nc.psum_tensor("mp", [128, 512], F32))

        for s in SEMS:
            P.sem[s] = E(nc.semaphore(s))
        blk = E(nc.Block())

        cc_in = cc_in_t.ap()
        cc_out = cc_out_t.ap()

        for k in range(8):
            dma2(winb[:, k * 4096:(k + 1) * 4096], winT[128 * k:128 * (k + 1), :],
                 deps=())
        dma2(wxb.ap().rearrange("p (c n) -> p c n", n=129),
             wxT.rearrange("(c p) n -> p c n", p=128), deps=())
        dma1(convbc[:, :], convb, deps=())
        dma1(dcolb[:, :], dcol, deps=())
        dma1(maskb[:, :], maskrow, deps=())
        dma1(sselb[:, :], slotsel, deps=())
        dma1(rselb[:, :], recvsel, deps=())
        dma1(arow[:, :], alog, deps=())

        ve(lambda e: e.memset(onecol[:], 1.0), drain=False)
        ve(lambda e: e.memset(onecol64[:], 1.0), drain=False)
        ve(lambda e: e.memset(onerow64[:], 1.0), drain=False)
        ve(lambda e: e.memset(ones128r[:], 1.0), drain=False)
        ve(lambda e: e.memset(mpair.ap(), 0.0), drain=False)
        ve(lambda e: e.memset(rdc[:], 1.0), drain=False)
        ve(lambda e: e.memset(hsb.ap(), 0.0), drain=False)
        ve(lambda e: e.memset(hfb.ap(), 0.0), drain=False)
        ve(lambda e: e.memset(epsb[:], 1e-5), drain=False)
        se(lambda e: e.activation(arow[:], arow[:], AF.Exp), deps=("d1",))
        ve(lambda e: e.tensor_scalar(arow[:], arow[:], -1.0, None, op0=ALU.mult),
           deps=("se",))

        for n in range(8):
            def mk(n=n):
                def f(e):
                    ins = None
                    for k in range(8):
                        ins = e.matmul(pp[0:1, 0:512], onecol[:, :],
                                       winb[:, k * 4096 + n * 512:k * 4096 + (n + 1) * 512],
                                       start=(k == 0), stop=(k == 7))
                    return ins
                return f
            pe(mk(), deps=("d2", "ve"))
            ve(lambda e, n=n: e.tensor_copy(wrow[0:1, n * 512:(n + 1) * 512],
                                            pp[0:1, 0:512]), deps=("pe",))
            pe(lambda e: e.nop(), deps=("ve",))

        def quant_tile(width):
            nblk = width // 16
            se(lambda e: e.activation(qa[:, 0:width], xnq[:, 0:width], AF.Abs),
               deps=("ve", "d1"), drain=False)
            se(lambda e: e.sign(qb[:, 0:width], xnq[:, 0:width]))
            ve(lambda e: e.reduce_max(
                qc[:, 0:nblk], qa[:, 0:width].rearrange("p (g b) -> p g b", b=16),
                axis=AX.X), deps=("se",))
            ve(lambda e: e.tensor_scalar(qc[:, nblk:2 * nblk], qc[:, 0:nblk],
                                         1.0 / 6.0, 1e-12, op0=ALU.mult, op1=ALU.max))
            ve(lambda e: e.reciprocal(qc[:, 2 * nblk:3 * nblk], qc[:, nblk:2 * nblk]))
            ve(lambda e: e.tensor_scalar(qc[:, 3 * nblk:4 * nblk], qc[:, nblk:2 * nblk],
                                         0.5, None, op0=ALU.mult))
            ve(lambda e: e.tensor_mul(
                qa[:, 0:width].rearrange("p (g b) -> p g b", b=16),
                qa[:, 0:width].rearrange("p (g b) -> p g b", b=16),
                qc[:, 2 * nblk:3 * nblk].unsqueeze(2).broadcast_to([128, nblk, 16])))
            ve(lambda e: e.tensor_scalar(xnq[:, 0:width], qa[:, 0:width],
                                         MIDS[0], None, op0=ALU.is_gt))
            for j, w in ((1, None), (2, None), (3, None), (4, 2.0), (5, 2.0), (6, 4.0)):
                if w is None:
                    ve(lambda e, j=j: e.tensor_scalar(xsq[:, 0:width], qa[:, 0:width],
                                                      MIDS[j], None, op0=ALU.is_gt))
                else:
                    ve(lambda e, j=j, w=w: e.tensor_scalar(
                        xsq[:, 0:width], qa[:, 0:width], MIDS[j], w,
                        op0=ALU.is_gt, op1=ALU.mult))
                ve(lambda e: e.tensor_add(xnq[:, 0:width], xnq[:, 0:width],
                                          xsq[:, 0:width]))
            ve(lambda e: e.tensor_mul(xnq[:, 0:width], xnq[:, 0:width], qb[:, 0:width]))
            ve(lambda e: e.tensor_mul(
                qo[:, 0:width].rearrange("p (g b) -> p g b", b=16),
                xnq[:, 0:width].rearrange("p (g b) -> p g b", b=16),
                qc[:, 3 * nblk:4 * nblk].unsqueeze(2).broadcast_to([128, nblk, 16])))

        for wsrc, wdst in ((wg, sc_wgq), (wu, sc_wuq)):
            for t in range(22):
                dma1(xnq[:, 0:1024], wsrc[128 * t:128 * (t + 1), :])
                quant_tile(1024)
                dma1(wdst[128 * t:128 * (t + 1), :], qo[:, 0:1024])
        for t in range(8):
            for p0, w in ((0, 1024), (1024, 1024), (2048, 768)):
                dma1(xnq[:, 0:w], wd[128 * t:128 * (t + 1), p0:p0 + w])
                quant_tile(w)
                dma1(sc_wdq[128 * t:128 * (t + 1), p0:p0 + w], qo[:, 0:w])

        def front_proj(c0, T, dst0, zc0, do_z, nf):
            for k in range(8):
                dma2(xTb[:, k * 512:k * 512 + T], xT[128 * k:128 * (k + 1), c0:c0 + T])

            def mms(e):
                ins = None
                for k in range(8):
                    ins = e.matmul(pp[0:1, 512:512 + T], onecol[:, :],
                                   xTb[:, k * 512:k * 512 + T],
                                   start=(k == 0), stop=(k == 7))
                return ins
            pe(mms, deps=("d2",))
            for k in range(8):
                se(lambda e, k=k: e.square(xsq[:, 0:T], xTb[:, k * 512:k * 512 + T]),
                   deps=("d2", "pe"))
                ve(lambda e: e.tensor_copy(qo[:, 0:T], xsq[:, 0:T]), deps=("se",))

                def mks(e, k=k):
                    return e.matmul(pp[32:33, 512:512 + T], onecol[:, :], qo[:, 0:T],
                                    start=(k == 0), stop=(k == 7))
                pe(mks, deps=("ve",))
            ve(lambda e: e.tensor_scalar(rows[0:1, 0:T], pp[0:1, 512:512 + T],
                                         1.0 / DM, None, op0=ALU.mult), deps=("pe",))
            ve(lambda e: e.tensor_scalar(rows[0:1, 512:512 + T], rows[0:1, 0:T],
                                         -1.0, None, op0=ALU.mult))
            ve(lambda e: e.tensor_mul(rows[0:1, 1024:1024 + T], rows[0:1, 0:T],
                                      rows[0:1, 0:T]))
            ve(lambda e: e.tensor_scalar(xsq[0:1, 0:T], pp[32:33, 512:512 + T],
                                         1.0 / DM, None, op0=ALU.mult))
            ve(lambda e: e.tensor_sub(rows[0:1, 1536:1536 + T], xsq[0:1, 0:T],
                                      rows[0:1, 1024:1024 + T]))
            se(lambda e: e.activation(rows[0:1, 1024:1024 + T], rows[0:1, 1536:1536 + T],
                                      AF.Sqrt, bias=epsb[0:1, 0:1], scale=1.0),
               deps=("ve",))
            ve(lambda e: e.reciprocal(rows[0:1, 1536:1536 + T], rows[0:1, 1024:1024 + T]),
               deps=("se",))
            ve(lambda e: e.tensor_copy(negmb[0:1, 0:T], rows[0:1, 512:512 + T]))
            pe(lambda e: e.matmul(pp[0:128, 0:T], ones128r[0:1, :],
                                  rows[0:1, 1536:1536 + T], start=True, stop=True),
               deps=("ve",))
            ve(lambda e: e.tensor_copy(qb[:, 0:T], pp[:, 0:T]), deps=("pe",))
            for f in range(nf):
                def mkf(f=f):
                    def g(e):
                        ins = None
                        for k in range(8):
                            ins = e.matmul(
                                mp[:, 0:T],
                                winb[:, k * 4096 + f * 128:k * 4096 + (f + 1) * 128],
                                xTb[:, k * 512:k * 512 + T],
                                start=(k == 0), stop=False)
                        return e.matmul(mp[:, 0:T], wrow[0:1, f * 128:(f + 1) * 128],
                                        negmb[0:1, 0:T], start=False, stop=True)
                    return g
                pe(mkf(), deps=("ve",))
                if f < 16:
                    ve(lambda e, f=f: e.tensor_mul(
                        xinb[:, f * 515 + dst0:f * 515 + dst0 + T],
                        mp[:, 0:T], qb[:, 0:T]), deps=("pe",))
                else:
                    ve(lambda e: e.tensor_mul(zstg[:, 0:T], mp[:, 0:T], qb[:, 0:T]),
                       deps=("pe",))
                    dma1(sc_z[128 * (f - 16):128 * (f - 15), zc0:zc0 + T],
                         zstg[:, 0:T])
                pe(lambda e: e.nop(), deps=("ve",))

        def front_seg(s):
            S0 = s * SEG
            if s == 0:
                front_proj(0, 3, 0, 0, do_z=False, nf=16)
            else:
                ve(lambda e: e.tensor_copy(
                    xinb.ap().rearrange("p (f c) -> p f c", c=515)[:, :, 0:3],
                    estash.ap().rearrange("p (f c) -> p f c", c=3)))
            front_proj(3 + S0, SEG, 3, S0, do_z=True, nf=32)
            for j in range(4):
                dma1(qa[:, 0:1024], xtm[S0 + 128 * j:S0 + 128 * (j + 1), :],
                     deps=("ve", "se", "pe", "d1"))
                xv = qa[:, 0:1024]
                ve(lambda e, xv=xv, j=j: e.reduce_sum(colb[:, 0:1], xv, axis=AX.X),
                   deps=("d1",))
                se(lambda e, xv=xv: e.square(xsq[:, 0:1024], xv))
                ve(lambda e: e.reduce_sum(colb[:, 1:2], xsq[:, 0:1024], axis=AX.X),
                   deps=("se",))
                ve(lambda e: e.tensor_scalar(colb[:, 2:3], colb[:, 0:1], 1.0 / DM,
                                             None, op0=ALU.mult))
                ve(lambda e: e.tensor_mul(colb[:, 3:4], colb[:, 2:3], colb[:, 2:3]))
                ve(lambda e: e.tensor_scalar(colb[:, 4:5], colb[:, 1:2], 1.0 / DM,
                                             None, op0=ALU.mult))
                ve(lambda e: e.tensor_sub(colb[:, 5:6], colb[:, 4:5], colb[:, 3:4]))
                se(lambda e: e.activation(colb[:, 6:7], colb[:, 5:6], AF.Sqrt,
                                          bias=epsb[:, 0:1], scale=1.0), deps=("ve",))
                ve(lambda e: e.reciprocal(colb[:, 7:8], colb[:, 6:7]), deps=("se",))
                ve(lambda e, xv=xv: e.tensor_scalar(
                    xnq[:, 0:1024], xv, colb[:, 2:3], colb[:, 7:8],
                    op0=ALU.subtract, op1=ALU.mult))
                quant_tile(1024)
                dma1(sc_xq[S0 + 128 * j:S0 + 128 * (j + 1), :], qo[:, 0:1024])
            ve(lambda e: e.tensor_copy(
                estash.ap().rearrange("p (f c) -> p f c", c=3),
                xinb.ap().rearrange("p (f c) -> p f c", c=515)[:, :, 512:515]))
            for f in range(16):
                dma2(convdbs[:, :], convd[:, f * 512:(f + 1) * 512], deps=("pe",))
                def mkc(f=f):
                    def g(e):
                        ins = None
                        for k in range(4):
                            ins = e.matmul(
                                mp[:, 0:512],
                                convdbs[:, k * 128:(k + 1) * 128],
                                xinb[:, f * 515 + k:f * 515 + k + 512],
                                start=(k == 0), stop=(k == 3))
                        return ins
                    return g
                pe(mkc(), deps=("ve", "d2"))
                se(lambda e, f=f: e.activation(xinb[:, f * 515 + 3:f * 515 + 515],
                                               mp[:, 0:512], AF.Silu,
                                               bias=convbc[:, f:f + 1], scale=1.0),
                   deps=("pe",))
                pe(lambda e: e.nop(), deps=("se",))
                dma1(sc_xc[128 * f:128 * (f + 1), S0:S0 + SEG],
                     xinb[:, f * 515 + 3:f * 515 + 515])
                if s == 0 and f == 0:
                    dma1(dbg[:, 1024:1536], xinb[:, 3:515])
            def mb(e):
                ins = None
                for c in range(16):
                    ins = e.matmul(pp[0:64, 0:512], wxb[:, c * 129:c * 129 + 64],
                                   xinb[:, c * 515 + 3:c * 515 + 515],
                                   start=(c == 0), stop=(c == 15))
                return ins
            pe(mb, deps=("se",))
            ve(lambda e: e.tensor_copy(bb[:, :], pp[0:64, 0:512]), deps=("pe",))

            def mc(e):
                ins = None
                for c in range(16):
                    ins = e.matmul(pp[64:128, 0:512], wxb[:, c * 129 + 64:c * 129 + 128],
                                   xinb[:, c * 515 + 3:c * 515 + 515],
                                   start=(c == 0), stop=(c == 15))
                return ins
            pe(mc, deps=("se",))
            ve(lambda e: e.tensor_copy(cb[:, :], pp[64:128, 0:512]), deps=("pe",))

            def md(e):
                ins = None
                for c in range(16):
                    ins = e.matmul(pp[0:1, 512:1024], wxb[:, c * 129 + 128:c * 129 + 129],
                                   xinb[:, c * 515 + 3:c * 515 + 515],
                                   start=(c == 0), stop=(c == 15))
                return ins
            pe(md, deps=("ve",))
            se(lambda e: e.activation(rows[0:1, 0:512], pp[0:1, 512:1024], AF.Abs),
               deps=("pe",))
            se(lambda e: e.activation(rows[0:1, 512:1024], rows[0:1, 0:512], AF.Exp,
                                      bias=0.0, scale=-1.0))
            ve(lambda e: e.tensor_scalar(rows[0:1, 512:1024], rows[0:1, 512:1024],
                                         1.0, None, op0=ALU.add), deps=("se",))
            se(lambda e: e.activation(rows[0:1, 1024:1536], rows[0:1, 512:1024], AF.Ln),
               deps=("ve",))
            ve(lambda e: e.tensor_scalar(dtrow[0:1, :], pp[0:1, 512:1024], 0.0, None,
                                         op0=ALU.max), deps=("se",))
            ve(lambda e: e.tensor_add(dtrow[0:1, :], dtrow[0:1, :],
                                      rows[0:1, 1024:1536]))
            pe(lambda e: e.nop(), deps=("ve",))

        def mid_seg(s):
            S0 = s * SEG
            dma1(sc_mid[0:1, 0:512], dtrow[0:1, :])
            dma1(dtcm[:, :], sc_mid[0:1, 0:512].rearrange("o (c t) -> (o c) t", t=64),
                 deps=("d1",))
            ve(lambda e: e.memset(t2cm[:, :], 0.0), deps=("d1",), drain=False)
            P.step("vector", lambda e: e.drain())
            ve(lambda e: e.tensor_tensor_scan(tlcm[:, :], dtcm[:, :], t2cm[:, :], 0.0,
                                              op0=ALU.add, op1=ALU.add), deps=("d1",))
            ve(lambda e: e.tensor_copy(t2cm[:, :], tlcm[:, :]))
            ve(lambda e: e.tensor_scalar(tlcm[:, 32:64], t2cm[:, 32:64],
                                         t2cm[:, 31:32], None, op0=ALU.subtract))
            ve(lambda e: e.tensor_scalar(rtlcm[:, 0:32], tlcm[:, 0:32], -1.0,
                                         tlcm[:, 31:32], op0=ALU.mult, op1=ALU.add))
            ve(lambda e: e.tensor_scalar(rtlcm[:, 32:64], tlcm[:, 32:64], -1.0,
                                         tlcm[:, 63:64], op0=ALU.mult, op1=ALU.add))
            ve(lambda e: e.tensor_sub(
                dlcm.ap().rearrange("p (ti a t) -> p ti a t", a=32, t=32),
                tlcm.ap().rearrange("p (ti t) -> p ti t", t=32).unsqueeze(2)
                .broadcast_to([8, 2, 32, 32]),
                tlcm.ap().rearrange("p (ti a) -> p ti a", a=32).unsqueeze(3)
                .broadcast_to([8, 2, 32, 32])))
            dma1(sc_mid[0:1, 1024:1536].rearrange("o (p t) -> (o p) t", t=64),
                 tlcm[:, :])
            dma1(sc_mid[0:1, 1536:2048].rearrange("o (p t) -> (o p) t", t=64),
                 rtlcm[:, :])
            dma1(sc_mid[0:1, 2048:18432].rearrange("o (p t) -> (o p) t", t=2048),
                 dlcm[:, :])
            dma1(sc_mid[0:1, 18432:18448].rearrange("o (p a) -> (o p) a", a=2).unsqueeze(2),
                 tlcm.ap().rearrange("p (a t) -> p a t", t=32)[:, :, 31:32],
                 allow_slow_non_contiguous=True)
            dma1(tlrow[0:1, :], sc_mid[0:1, 1024:1536], deps=("d1",))
            dma1(rtlrow[0:1, :], sc_mid[0:1, 1536:2048], deps=("d1",))
            dma1(strow[0:1, :], sc_mid[0:1, 18432:18448], deps=("d1",))
            pe(lambda e: e.matmul(mp[0:64, 0:512], arow[:, :], tlrow[0:1, :],
                                  start=True, stop=True), deps=("d1",))
            se(lambda e: e.activation(einb[:, :], mp[0:64, 0:512], AF.Exp),
               deps=("pe",))
            pe(lambda e: e.matmul(mp[0:64, 0:512], arow[:, :], rtlrow[0:1, :],
                                  start=True, stop=True), deps=("se",))
            se(lambda e: e.activation(eodb[:, :], mp[0:64, 0:512], AF.Exp),
               deps=("pe",))
            pe(lambda e: e.matmul(mp[0:64, 0:512], onerow64[:, :], dtrow[0:1, :],
                                  start=True, stop=True), deps=("se",))
            ve(lambda e: e.tensor_mul(eodb[:, :], eodb[:, :], mp[0:64, 0:512]),
               deps=("pe",))
            ve(lambda e: e.tensor_mul(bdt[:, :], bb[:, :], mp[0:64, 0:512]))
            ve(lambda e: e.tensor_copy(cbb[:, :], cb[:, :]))
            pe(lambda e: e.matmul(mp[0:64, 0:16], arow[:, :], strow[0:1, :],
                                  start=True, stop=True), deps=("ve",))
            se(lambda e: e.activation(lamt[:, :], mp[0:64, 0:16], AF.Exp), deps=("pe",))
            ve(lambda e: e.tensor_mul(lamp[:, :], lamt.ap()[:, 0:16:2],
                                      lamt.ap()[:, 1:16:2]), deps=("se",))
            ve(lambda e: e.tensor_mul(bt[:, :], bb[:, :], eodb[:, :]))
            ve(lambda e: e.tensor_mul(ce[:, :], cb[:, :], einb[:, :]))
            for p in range(8):
                ve(lambda e, p=p: e.tensor_copy(cx[:, 64 * p:64 * p + 32],
                                                ce[:, 64 * p:64 * p + 32]), drain=False)
                ve(lambda e, p=p: e.tensor_scalar(
                    cx[:, 64 * p + 32:64 * p + 64], ce[:, 64 * p + 32:64 * p + 64],
                    lamt[:, 2 * p:2 * p + 1], None, op0=ALU.mult))
                ve(lambda e, p=p: e.tensor_scalar(
                    cedstg[:, :], cx[:, 64 * p:64 * (p + 1)],
                    rdc[:, 0:1], None, op0=ALU.mult))
                dma1(sc_ced[:, S0 + 64 * p:S0 + 64 * (p + 1)], cedstg[:, :])
                ve(lambda e, p=p: e.tensor_mul(rdc[:, 0:1], rdc[:, 0:1],
                                               lamp[:, p:p + 1]))
                ve(lambda e, p=p: e.tensor_scalar(
                    wfm[:, 64 * p:64 * p + 32], bt[:, 64 * p:64 * p + 32],
                    lamt[:, 2 * p + 1:2 * p + 2], None, op0=ALU.mult), drain=False)
                ve(lambda e, p=p: e.tensor_copy(wfm[:, 64 * p + 32:64 * p + 64],
                                                bt[:, 64 * p + 32:64 * p + 64]))
            dma1(sc_W[:, :], wfm[:, :])
            for g in range(4):
                dma1t(wtm[:, g * 64:(g + 1) * 64],
                      sc_W[:, 128 * g:128 * (g + 1)], deps=("d1",))
            for p in range(8):
                if p % 2 == 0:
                    g = p // 2
                    dma1t(xctm[:, 0:2048],
                          sc_xc[:, S0 + 128 * g:S0 + 128 * (g + 1)],
                          deps=("d1", "pe"))
                chunk_64(s, p)

        def chunk_64(s, p):
            S0 = s * SEG
            glob = s * 8 + p
            g2 = 0
            roff = 64 * (p % 2)
            dma1(dlrow[0:1, :], sc_mid[0:1, 2048 + p * 2048:2048 + (p + 1) * 2048],
                 deps=("d1", "pe"))
            for ti in range(2):
                def mkp(ti=ti, p=p):
                    def g(e):
                        base = ti * 1024
                        e.matmul(pp[0:64, 0:512], arow[:, :],
                                 dlrow[0:1, base:base + 512], start=True, stop=False)
                        e.matmul(pp[0:64, 0:512], onerow64[:, :], maskb[0:1, 0:512],
                                 start=False, stop=True)
                        e.matmul(pp[0:64, 512:1024], arow[:, :],
                                 dlrow[0:1, base + 512:base + 1024], start=True,
                                 stop=False)
                        return e.matmul(pp[0:64, 512:1024], onerow64[:, :],
                                        maskb[0:1, 512:1024], start=False, stop=True)
                    return g
                pe(mkp(), deps=("ve", "d1", "se"))
                se(lambda e: e.activation(ebuf[:, :], pp[0:64, 0:1024], AF.Exp),
                   deps=("pe",))
                tb = 64 * p + 32 * ti
                ve(lambda e, tb=tb: e.tensor_mul(
                    ebuf.ap().rearrange("p (a t) -> p a t", t=32),
                    ebuf.ap().rearrange("p (a t) -> p a t", t=32),
                    bdt[:, tb:tb + 32].unsqueeze(2).broadcast_to([64, 32, 32])),
                   deps=("se",))
                ve(lambda e, tb=tb: e.tensor_mul(
                    qbufb.ap().rearrange("p (a t) -> p a t", t=32),
                    ebuf.ap().rearrange("p (a t) -> p a t", t=32),
                    cbb[:, tb:tb + 32].unsqueeze(1).broadcast_to([64, 32, 32])))
                def mkr(e):
                    e.matmul(pp[0:1, 0:512], onecol64[:, :], qbufb[:, 0:512],
                             start=True, stop=True)
                    return e.matmul(pp[0:1, 512:1024], onecol64[:, :],
                                    qbufb[:, 512:1024], start=True, stop=True)
                pe(mkr, deps=("ve",))
                se(lambda e: e.activation(mrow[0:1, :], pp[0:1, 0:1024], AF.Identity)
                   if False else e.copy(mrow[0:1, :], pp[0:1, 0:1024]), deps=("pe",))
                dma1(sc_M[0:1, :], mrow[0:1, :])
                dma1(mpair[roff + 32 * ti:roff + 32 * ti + 32, 32 * ti:32 * ti + 32],
                     sc_M[0:1, :].rearrange("o (a t) -> (o a) t", t=32), deps=("d1",))
            pe(lambda e, p=p: e.matmul(mp[0:32, 0:32], bt[:, 64 * p:64 * p + 32],
                                       ce[:, 64 * p + 32:64 * p + 64],
                                       start=True, stop=True), deps=("ve", "d1"))
            se(lambda e: e.copy(modb[:, :], mp[0:32, 0:32]), deps=("pe",))
            dma2(mpair[roff:roff + 32, 32:64], modb[:, :], deps=("se",))
            for dt_ in range(4):
                def mk2(dt_=dt_, p=p, g2=g2, roff=roff, glob=glob):
                    def g(e):
                        ins = e.matmul(yp[0:64, 0:512], mpair[roff:roff + 64, :],
                                       xctm[roff:roff + 64,
                                            g2 * 2048 + dt_ * 512:g2 * 2048 + (dt_ + 1) * 512],
                                       start=True, stop=(glob == 0))
                        if glob > 0:
                            e.matmul(yp[0:32, 0:512], ce[:, 64 * p:64 * p + 32],
                                     hsb[:, dt_ * 512:(dt_ + 1) * 512],
                                     start=False, stop=False)
                            ins = e.matmul(yp[32:64, 0:512],
                                           cx[:, 64 * p + 32:64 * p + 64],
                                           hsb[:, dt_ * 512:(dt_ + 1) * 512],
                                           start=False, stop=True)
                        return ins
                    return g
                pe(mk2(), deps=("d2", "ve", "d1"))
                ve(lambda e: e.tensor_copy(ytm[:, 0:512], yp[0:64, 0:512]),
                   deps=("pe",))
                pe(lambda e: e.nop(), deps=("ve",))
                dma1(sc_y[S0 + 64 * p:S0 + 64 * (p + 1), dt_ * 512:(dt_ + 1) * 512],
                     ytm[:, 0:512])
            def mks(p=p, g2=g2, roff=roff):
                def g(e):
                    ins = None
                    for dt_ in range(4):
                        ins = e.matmul(hp[:, dt_ * 512:(dt_ + 1) * 512],
                                       wtm[roff:roff + 64,
                                           (p // 2) * 64:(p // 2) * 64 + 64],
                                       xctm[roff:roff + 64,
                                            g2 * 2048 + dt_ * 512:g2 * 2048 + (dt_ + 1) * 512],
                                       start=True, stop=True)
                    return ins
                return g
            pe(mks(), deps=("ve", "d1"))
            if glob > 0:
                ve(lambda e, p=p: e.scalar_tensor_tensor(
                    hfb[:, :], hfb[:, :], lamp[:, p:p + 1], hp[:, :],
                    op0=ALU.mult, op1=ALU.add), deps=("pe",))
            else:
                ve(lambda e: e.tensor_copy(hfb[:, :], hp[:, :]), deps=("pe",))
            ve(lambda e: e.tensor_copy(hsb[:, :], hfb[:, :]))
            pe(lambda e: e.nop(), deps=("ve",))

        for s in range(NSEG):
            front_seg(s)
            mid_seg(s)

        for j in range(4):
            ve(lambda e, j=j: e.tensor_scalar(ccst[:, :], hfb[:, :],
                                              sselb[:, j:j + 1], None, op0=ALU.mult))
            dma1(cc_in[64 * j:64 * (j + 1), :], ccst[:, :])
        P.step("gpsimd",
               lambda e: e.collective_compute(
                   "AllReduce", ALU.add, replica_groups=[list(range(8))],
                   ins=[cc_in.opt()], outs=[cc_out.opt()]),
               deps=("d1",), inc="cc", n=1)

        barrier()
        es1.close()
        es2 = ExitStack()
        E2 = es2.enter_context

        def S2(name, shape, dt=F32):
            return E2(nc.sbuf_tensor(name, shape, dt))

        xqTs = S2("xqTs", [128, 8 * 4096], BF16)
        wstgA = S2("wstgA", [128, 8 * 512], BF16)
        wstgB = S2("wstgB", [128, 8 * 512], BF16)
        gsilA = S2("gsilA", [128, 512])

        for k in range(8):
            dma1t(xqTs[:, k * 4096:(k + 1) * 4096],
                  sc_xq[:, 128 * k:128 * (k + 1)], deps=("d1",))
        NT = SPAN // 128
        for nchunk in range(6):
            n0 = nchunk * 512
            nw = min(512, IF - n0)
            for k in range(8):
                dma1t(wstgA[:, k * 512:k * 512 + nw],
                      sc_wgq[n0:n0 + nw, 128 * k:128 * (k + 1)], deps=("d1", "pe"))
                dma1t(wstgB[:, k * 512:k * 512 + nw],
                      sc_wuq[n0:n0 + nw, 128 * k:128 * (k + 1)], deps=("d1", "pe"))
            for t in range(NT):
                def mg(t=t, nw=nw):
                    def g(e):
                        ins = None
                        for k in range(8):
                            ins = e.matmul(yp[:, 0:nw],
                                           xqTs[:, k * 4096 + t * 128:k * 4096 + (t + 1) * 128],
                                           wstgA[:, k * 512:k * 512 + nw],
                                           start=(k == 0), stop=(k == 7))
                        return ins
                    return g
                pe(mg(), deps=("d1",))
                se(lambda e, nw=nw: e.activation(gsilA[:, 0:nw], yp[:, 0:nw], AF.Silu),
                   deps=("pe",))

                def mu(t=t, nw=nw):
                    def g(e):
                        ins = None
                        for k in range(8):
                            ins = e.matmul(mp[:, 0:nw],
                                           xqTs[:, k * 4096 + t * 128:k * 4096 + (t + 1) * 128],
                                           wstgB[:, k * 512:k * 512 + nw],
                                           start=(k == 0), stop=(k == 7))
                        return ins
                    return g
                pe(mu(), deps=("se",))
                ve(lambda e, nw=nw: e.tensor_mul(xnq[:, 0:nw], gsilA[:, 0:nw],
                                                 mp[:, 0:nw]), deps=("pe", "se"))
                pe(lambda e: e.nop(), deps=("ve",))
                quant_tile(nw)
                dma1(sc_hq[128 * t:128 * (t + 1), n0:n0 + nw], qo[:, 0:nw])
                if t == 0 and nchunk == 0:
                    dma1(dbg[:, 1536:2048], qo[:, 0:512])
                    dma2(dbg[:, 3072:3584], xnq[:, 0:512])

        barrier()
        es2.close()
        es3 = ExitStack()
        E3 = es3.enter_context

        def S3(name, shape, dt=F32):
            return E3(nc.sbuf_tensor(name, shape, dt))

        woutb = S3("woutb", [128, 16 * 1024], BF16)
        wdqb = S3("wdqb", [128, 22 * 1024], BF16)
        hqTs = S3("hqTs", [128, 22 * 512], BF16)
        ygfm = S3("ygfm", [128, 16 * 512], BF16)
        yfmt = S3("yfmt", [128, 512], BF16)
        xctb = S3("xctb", [128, 512], BF16)
        zstg2 = S3("zstg2", [128, 512], BF16)
        gsilB = S3("gsilB", [128, 512])
        hrecv = S3("hrecv", [64, 2048])
        hrecvb = S3("hrecvb", [64, 2048], BF16)
        outsb = xnq
        xres = qa
        cedt = S3("cedt", [64, 512], BF16)

        ve(lambda e: e.memset(hrecv[:, :], 0.0), deps=("cc",))
        for j in range(4):
            dma1(ccst[:, :], cc_out[64 * j:64 * (j + 1), :], deps=("cc", "ve"))
            ve(lambda e, j=j: e.tensor_scalar(hfb[:, :], ccst[:, :],
                                              rselb[:, j:j + 1], None, op0=ALU.mult),
               deps=("d1",))
            ve(lambda e: e.tensor_add(hrecv[:, :], hrecv[:, :], hfb[:, :]))
        ve(lambda e: e.tensor_copy(hrecvb[:, :], hrecv[:, :]))

        for k in range(16):
            dma2(woutb[:, k * 1024:(k + 1) * 1024], woutT[128 * k:128 * (k + 1), :],
                 deps=("pe", "ve"))
        for i in range(22):
            dma1t(wdqb[:, i * 1024:(i + 1) * 1024],
                  sc_wdq[:, 128 * i:128 * (i + 1)], deps=("d1", "pe"))
        for s in range(NSEG):
            S0 = s * SEG
            dma1(cedt[:, :], sc_ced[:, S0:S0 + SEG], deps=("ve", "pe"))
            for f in range(16):
                dma1t(yfmt[:, :], sc_y[S0:S0 + SEG, 128 * f:128 * (f + 1)],
                      deps=("d1", "ve"))
                dma1(xctb[:, :], sc_xc[128 * f:128 * (f + 1), S0:S0 + SEG])
                dma1(zstg2[:, :], sc_z[128 * f:128 * (f + 1), S0:S0 + SEG])

                pe(lambda e, f=f: e.matmul(mp[:, 0:512],
                                           hrecvb[:, f * 128:(f + 1) * 128],
                                           cedt[:, :],
                                           start=True, stop=True),
                   deps=("ve", "d1"))
                ve(lambda e: e.tensor_copy(qa[:, 0:512], yfmt[:, :]), deps=("d1",))
                ve(lambda e: e.tensor_add(qa[:, 0:512], qa[:, 0:512], mp[:, 0:512]),
                   deps=("pe",))
                pe(lambda e: e.nop(), deps=("ve",))
                ve(lambda e, f=f: e.scalar_tensor_tensor(
                    qb[:, 0:512], xctb[:, :], dcolb[:, f:f + 1], qa[:, 0:512],
                    op0=ALU.mult, op1=ALU.add), deps=("d1",))
                se(lambda e: e.activation(gsilB[:, 0:512], zstg2[:, :], AF.Silu),
                   deps=("d1",))
                ve(lambda e, f=f: e.tensor_mul(ygfm[:, f * 512:(f + 1) * 512],
                                               qb[:, 0:512], gsilB[:, 0:512]),
                   deps=("se",))
            for i in range(22):
                dma1t(hqTs[:, i * 512:(i + 1) * 512],
                      sc_hq[S0:S0 + SEG, 128 * i:128 * (i + 1)], deps=("d1", "pe"))
            for t4 in range(4):
                T0 = S0 + 128 * t4
                dma1(xres[:, :], xtm[T0:T0 + 128, :])
                for nh in range(2):
                    def mo(t4=t4, nh=nh):
                        def g(e):
                            ins = None
                            for k in range(16):
                                ins = e.matmul(
                                    yp[:, 0:512],
                                    ygfm[:, (k // 4) * 2048 + 0:0] if False else
                                    ygfm[:, k * 512 + 0:k * 512 + 0] if False else
                                    ygfm.ap().rearrange("p (f t) -> p f t", t=512)
                                    [:, k, 128 * t4:128 * t4 + 128],
                                    woutb[:, k * 1024 + nh * 512:k * 1024 + (nh + 1) * 512],
                                    start=(k == 0), stop=False)
                            for i in range(22):
                                ins = e.matmul(
                                    yp[:, 0:512],
                                    hqTs[:, i * 512 + 128 * t4:i * 512 + 128 * (t4 + 1)],
                                    wdqb[:, i * 1024 + nh * 512:i * 1024 + (nh + 1) * 512],
                                    start=False, stop=(i == 21))
                            return ins
                        return g
                    pe(mo(), deps=("ve", "d1", "d2"))
                    ve(lambda e, nh=nh: e.tensor_add(outsb[:, nh * 512:(nh + 1) * 512],
                                                     xres[:, nh * 512:(nh + 1) * 512],
                                                     yp[:, 0:512]), deps=("pe", "d1"))
                    pe(lambda e: e.nop(), deps=("ve",))
                dma1(out[T0:T0 + 128, :], outsb[:, :])

        P.step("sync", lambda e: e.wait_ge(P.sem["d1"], P.cnt["d1"]))
        P.step("gpsimd", lambda e: e.wait_ge(P.sem["d2"], P.cnt["d2"]))

        @blk.sync
        def _(eng):
            for en, fn in P.steps:
                if en == "sync":
                    fn(eng)

        @blk.vector
        def _(eng):
            for en, fn in P.steps:
                if en == "vector":
                    fn(eng)

        @blk.scalar
        def _(eng):
            for en, fn in P.steps:
                if en == "scalar":
                    fn(eng)

        @blk.tensor
        def _(eng):
            for en, fn in P.steps:
                if en == "tensor":
                    fn(eng)

        @blk.gpsimd
        def _(eng):
            for en, fn in P.steps:
                if en == "gpsimd":
                    fn(eng)

        es3.close()

    return nc


def stage_inputs(inputs, c):
    b, h = c // 2, c % 2
    x = np.asarray(inputs["L0_attn_x"])
    f32 = np.float32
    xb = np.ascontiguousarray(x[b])
    s0 = h * SPAN
    xTfull = np.ascontiguousarray(xb.T)
    halo = np.zeros((DM, 3), f32) if h == 0 else xTfull[:, s0 - 3:s0]
    xTs = np.concatenate([halo, xTfull[:, s0:s0 + SPAN]], axis=1)
    xtm_s = np.ascontiguousarray(xb[s0:s0 + SPAN])
    cw = np.asarray(inputs["L0_attn_conv_w"]).reshape(DI, 4)
    cdd = np.zeros((4, 16, 128, 128), f32)
    for k in range(4):
        for f in range(16):
            np.fill_diagonal(cdd[k, f], cw[128 * f:128 * (f + 1), k])
    convd_s = np.ascontiguousarray(cdd.transpose(2, 1, 0, 3).reshape(128, 64 * 128))
    convb_s = np.ascontiguousarray(
        np.asarray(inputs["L0_attn_conv_b"]).reshape(16, 128).T)
    dcol_s = np.ascontiguousarray(np.asarray(inputs["L0_attn_D"]).reshape(16, 128).T)
    mr = np.zeros((1, 1024), f32)
    for a in range(32):
        for t in range(32):
            if t < a:
                mr[0, 32 * a + t] = -30000.0
    ss = np.zeros((64, 4), f32)
    rs = np.zeros((64, 4), f32)
    if h == 0:
        ss[:, b] = 1.0
    else:
        rs[:, b] = 1.0
    return {
        "xT": np.ascontiguousarray(xTs, f32),
        "xtm": np.ascontiguousarray(xtm_s, f32),
        "winT": np.ascontiguousarray(np.asarray(inputs["L0_attn_W_in"]).T, f32),
        "wxT": np.ascontiguousarray(np.asarray(inputs["L0_attn_W_x"]).T, f32),
        "convd": convd_s.astype(f32),
        "convb": convb_s.astype(f32),
        "alog": np.asarray(inputs["L0_attn_A_log"], f32).reshape(1, DS),
        "dcol": dcol_s.astype(f32),
        "woutT": np.ascontiguousarray(np.asarray(inputs["L0_attn_W_out"]).T, f32),
        "wg": np.asarray(inputs["L0_ffn_w_gate"], f32),
        "wu": np.asarray(inputs["L0_ffn_w_up"], f32),
        "wd": np.asarray(inputs["L0_ffn_w_down"], f32),
        "maskrow": mr,
        "slotsel": ss,
        "recvsel": rs,
    }


def kernel(**inputs):
    nc = build_program()
    in_maps = [stage_inputs(inputs, c) for c in range(8)]
    res = run_bass_kernel_spmd(nc, in_maps, core_ids=list(range(8)))
    outp = np.zeros((B, L, DM), np.float32)
    for c in range(8):
        b, h = c // 2, c % 2
        outp[b, h * SPAN:(h + 1) * SPAN, :] = np.asarray(res.results[c]["out"])
    return outp


# version 05 (final)
# speedup vs baseline: 1.0437x; 1.0141x over previous
import numpy as np
from contextlib import ExitStack

import concourse.bass as bass
import concourse.mybir as mybir
from concourse.bass_utils import run_bass_kernel_spmd

F32 = mybir.dt.float32
BF16 = mybir.dt.bfloat16
AF = mybir.ActivationFunctionType
ALU = mybir.AluOpType
AX = mybir.AxisListType

B, L, DM = 4, 8192, 1024
DI, DS = 2048, 64
IF = 2816
SPAN, SEG = 4096, 512
NSEG = SPAN // SEG
MIDS = [0.25, 0.75, 1.25, 1.75, 2.5, 3.5, 5.0]

ENGS = ["sync", "vector", "scalar", "tensor", "gpsimd"]
SEMS = ["d1", "d2", "ve", "se", "pe", "cc", "st"]


class Plan:
    def __init__(self):
        self.steps = []
        self.cnt = {s: 0 for s in SEMS}
        self.seen = {e: {s: 0 for s in SEMS} for e in ENGS}
        self.sem = {}

    def step(self, eng, fn, deps=(), inc=None, n=1):
        if eng in ("vector", "scalar"):
            deps = tuple(deps) + ("d1", "d2")
        waits = []
        for s in deps:
            v = self.cnt[s]
            if v > self.seen[eng][s]:
                waits.append((s, v))
                self.seen[eng][s] = v
        if inc is not None:
            self.cnt[inc] += n
        sem = self.sem

        def run(e, waits=waits, fn=fn, inc=inc, n=n):
            for s, v in waits:
                e.wait_ge(sem[s], v)
            ins = fn(e)
            if inc is not None:
                ins.then_inc(sem[inc], n)
        self.steps.append((eng, run))


P = Plan()


def dma1(o, i, deps=("ve", "se", "pe"), **kw):
    P.step("sync", lambda e, o=o, i=i, kw=kw: e.dma_start(out=o, in_=i, **kw),
           deps=deps, inc="d1", n=16)


def dma1t(o, i, deps=("d1", "d2")):
    P.step("sync", lambda e, o=o, i=i: e.dma_start(out=o, in_=i, transpose=True),
           deps=deps, inc="d1", n=16)


def dma2(o, i, deps=("ve", "se", "pe"), **kw):
    P.step("gpsimd", lambda e, o=o, i=i, kw=kw: e.dma_start(out=o, in_=i, **kw),
           deps=deps, inc="d2", n=16)


def ve(fn, deps=(), drain=True):
    P.step("vector", fn, deps=deps, inc="ve")
    if drain:
        P.step("vector", lambda e: e.drain())


def se(fn, deps=(), drain=True):
    P.step("scalar", fn, deps=deps, inc="se")
    if drain:
        P.step("scalar", lambda e: e.drain())


def pe(fn, deps=()):
    P.step("tensor", fn, deps=deps, inc="pe")


def barrier():
    allsems = ("d1", "d2", "ve", "se", "pe", "cc")
    for eng in ENGS:
        P.step(eng, lambda e: e.nop(), deps=allsems)


def build_program():
    global P
    P = Plan()
    nc = bass.Bass("TRN2", target_bir_lowering=False)

    def din(name, shape):
        return nc.dram_tensor(name, shape, F32, kind="ExternalInput").ap()

    xT = din("xT", [DM, SPAN + 3])
    xtm = din("xtm", [SPAN, DM])
    winT = din("winT", [DM, 2 * DI])
    wxT = din("wxT", [DI, 129])
    convd = din("convd", [128, 64 * 128])
    convb = din("convb", [128, 16])
    alog = din("alog", [1, DS])
    dcol = din("dcol", [128, 16])
    woutT = din("woutT", [DI, DM])
    wg = din("wg", [IF, DM])
    wu = din("wu", [IF, DM])
    wd = din("wd", [DM, IF])
    maskrow = din("maskrow", [1, 1024])
    slotsel = din("slotsel", [64, 4])
    recvsel = din("recvsel", [64, 4])
    out = nc.dram_tensor("out", [SPAN, DM], F32, kind="ExternalOutput").ap()
    dbg = nc.dram_tensor("dbg", [128, 4096], BF16, kind="ExternalOutput").ap()

    def dsc(name, shape, dt=BF16, **kw):
        return nc.dram_tensor(name, shape, dt, kind="Internal", **kw)

    sc_xc = dsc("sc_xc", [DI, SPAN]).ap()
    sc_z = dsc("sc_z", [DI, SPAN]).ap()
    sc_y = dsc("sc_y", [SPAN, DI]).ap()
    sc_xq = dsc("sc_xq", [SPAN, DM]).ap()
    sc_hq = dsc("sc_hq", [SPAN, IF]).ap()
    sc_wgq = dsc("sc_wgq", [IF, DM]).ap()
    sc_wuq = dsc("sc_wuq", [IF, DM]).ap()
    sc_wdq = dsc("sc_wdq", [DM, IF]).ap()
    sc_M = dsc("sc_M", [1, 1024]).ap()
    sc_W = dsc("sc_W", [64, SEG]).ap()
    sc_mid = dsc("sc_mid", [1, 20000], F32).ap()
    sc_ced = dsc("sc_ced", [64, SPAN]).ap()
    cc_in_t = dsc("cc_in", [4 * 64, DI], F32)
    cc_out_t = dsc("cc_out", [4 * 64, DI], F32, addr_space="Shared")

    with ExitStack() as es:
        E = es.enter_context

        def S(name, shape, dt=F32):
            return E(nc.sbuf_tensor(name, shape, dt))

        wxb = S("wxb", [128, 16 * 129], BF16)
        convbc = S("convbc", [128, 16])
        dcolb = S("dcolb", [128, 16])
        xsq = S("xsq", [128, 1024])
        xnq = S("xnq", [128, 1024])
        qa = S("qa", [128, 1024])
        qb = S("qb", [128, 1024])
        qc = S("qc", [128, 600])
        qo = S("qo", [128, 1024], BF16)
        colb = S("colb", [128, 8])
        rows = S("rows", [1, 2048])
        wrow = S("wrow", [1, 4096], BF16)
        onecol = S("onecol", [128, 1], BF16)
        onecol64 = S("onecol64", [64, 1], BF16)
        onerow64 = S("onerow64", [1, 64])
        ones128r = S("ones128r", [1, 128])
        arow = S("arow", [1, 64])
        maskb = S("maskb", [1, 1024])
        sselb = S("sselb", [64, 4])
        rselb = S("rselb", [64, 4])
        negmb = S("negmb", [1, 512], BF16)
        cedstg = S("cedstg", [64, 64], BF16)
        hsb = S("hsb", [64, 2048], BF16)
        hfb = S("hfb", [64, 2048])
        dtrow = S("dtrow", [1, 512])
        tlrow = S("tlrow", [1, 512])
        rtlrow = S("rtlrow", [1, 512])
        strow = S("strow", [1, 16])
        mrow = S("mrow", [1, 1024], BF16)
        estash = S("estash", [128, 48], BF16)
        ccst = S("ccst", [64, 2048])
        epsb = S("epsb", [128, 1])
        es1 = ExitStack()
        E1 = es1.enter_context

        def S1(name, shape, dt=F32):
            return E1(nc.sbuf_tensor(name, shape, dt))

        winb = S1("winb", [128, 8 * 4096], BF16)
        convdbs = S1("convdbs", [128, 512], BF16)
        xTb = S1("xTb", [128, 8 * 512], BF16)
        xinb = S1("xinb", [128, 16 * 515], BF16)
        zstg = S1("zstg", [128, 512], BF16)
        bb = S1("bb", [64, 512])
        cb = S1("cb", [64, 512])
        dtcm = S1("dtcm", [8, 64])
        tlcm = S1("tlcm", [8, 64])
        t2cm = S1("t2cm", [8, 64])
        rtlcm = S1("rtlcm", [8, 64])
        dlcm = S1("dlcm", [8, 2048])
        dlrow = S1("dlrow", [1, 2048])
        einb = S1("einb", [64, 512])
        eodb = S1("eodb", [64, 512])
        bt = S1("bt", [64, 512], BF16)
        ce = S1("ce", [64, 512], BF16)
        cx = S1("cx", [64, 512], BF16)
        bdt = S1("bdt", [64, 512], BF16)
        cbb = S1("cbb", [64, 512], BF16)
        wfm = S1("wfm", [64, 512], BF16)
        lamt = S1("lamt", [64, 16])
        lamp = S1("lamp", [64, 8])
        rdc = S1("rdc", [64, 1])
        ebuf = S1("ebuf", [64, 1024], BF16)
        qbufb = S1("qbufb", [64, 1024], BF16)
        mpair = S1("mpair", [128, 64], BF16)
        modb = S1("modb", [32, 32], BF16)
        xctm = S1("xctm", [128, 2048], BF16)
        wtm = S1("wtm", [128, 4 * 64], BF16)
        ytm = S1("ytm", [64, 512], BF16)
        hp = E(nc.psum_tensor("hp", [64, 2048], F32))
        pp = E(nc.psum_tensor("pp", [128, 1024], F32))
        yp = E(nc.psum_tensor("yp", [128, 512], F32))
        mp = E(nc.psum_tensor("mp", [128, 512], F32))

        for s in SEMS:
            P.sem[s] = E(nc.semaphore(s))
        blk = E(nc.Block())

        cc_in = cc_in_t.ap()
        cc_out = cc_out_t.ap()

        for k in range(8):
            dma2(winb[:, k * 4096:(k + 1) * 4096], winT[128 * k:128 * (k + 1), :],
                 deps=())
        dma2(wxb.ap().rearrange("p (c n) -> p c n", n=129),
             wxT.rearrange("(c p) n -> p c n", p=128), deps=())
        dma1(convbc[:, :], convb, deps=())
        dma1(dcolb[:, :], dcol, deps=())
        dma1(maskb[:, :], maskrow, deps=())
        dma1(sselb[:, :], slotsel, deps=())
        dma1(rselb[:, :], recvsel, deps=())
        dma1(arow[:, :], alog, deps=())

        ve(lambda e: e.memset(onecol[:], 1.0), drain=False)
        ve(lambda e: e.memset(onecol64[:], 1.0), drain=False)
        ve(lambda e: e.memset(onerow64[:], 1.0), drain=False)
        ve(lambda e: e.memset(ones128r[:], 1.0), drain=False)
        ve(lambda e: e.memset(mpair.ap(), 0.0), drain=False)
        ve(lambda e: e.memset(rdc[:], 1.0), drain=False)
        ve(lambda e: e.memset(hsb.ap(), 0.0), drain=False)
        ve(lambda e: e.memset(hfb.ap(), 0.0), drain=False)
        ve(lambda e: e.memset(epsb[:], 1e-5), drain=False)
        se(lambda e: e.activation(arow[:], arow[:], AF.Exp), deps=("d1",))
        ve(lambda e: e.tensor_scalar(arow[:], arow[:], -1.0, None, op0=ALU.mult),
           deps=("se",))

        for n in range(8):
            def mk(n=n):
                def f(e):
                    ins = None
                    for k in range(8):
                        ins = e.matmul(pp[0:1, 0:512], onecol[:, :],
                                       winb[:, k * 4096 + n * 512:k * 4096 + (n + 1) * 512],
                                       start=(k == 0), stop=(k == 7))
                    return ins
                return f
            pe(mk(), deps=("d2", "ve"))
            ve(lambda e, n=n: e.tensor_copy(wrow[0:1, n * 512:(n + 1) * 512],
                                            pp[0:1, 0:512]), deps=("pe",))
            pe(lambda e: e.nop(), deps=("ve",))

        def quant_tile(width):
            nblk = width // 16
            se(lambda e: e.activation(qa[:, 0:width], xnq[:, 0:width], AF.Abs),
               deps=("ve", "d1"), drain=False)
            se(lambda e: e.sign(qb[:, 0:width], xnq[:, 0:width]))
            ve(lambda e: e.reduce_max(
                qc[:, 0:nblk], qa[:, 0:width].rearrange("p (g b) -> p g b", b=16),
                axis=AX.X), deps=("se",))
            ve(lambda e: e.tensor_scalar(qc[:, nblk:2 * nblk], qc[:, 0:nblk],
                                         1.0 / 6.0, 1e-12, op0=ALU.mult, op1=ALU.max))
            ve(lambda e: e.reciprocal(qc[:, 2 * nblk:3 * nblk], qc[:, nblk:2 * nblk]))
            ve(lambda e: e.tensor_scalar(qc[:, 3 * nblk:4 * nblk], qc[:, nblk:2 * nblk],
                                         0.5, None, op0=ALU.mult))
            ve(lambda e: e.tensor_mul(
                qa[:, 0:width].rearrange("p (g b) -> p g b", b=16),
                qa[:, 0:width].rearrange("p (g b) -> p g b", b=16),
                qc[:, 2 * nblk:3 * nblk].unsqueeze(2).broadcast_to([128, nblk, 16])))
            ve(lambda e: e.tensor_scalar(xnq[:, 0:width], qa[:, 0:width],
                                         MIDS[0], None, op0=ALU.is_gt))
            for j, w in ((1, None), (2, None), (3, None), (4, 2.0), (5, 2.0), (6, 4.0)):
                if w is None:
                    ve(lambda e, j=j: e.tensor_scalar(xsq[:, 0:width], qa[:, 0:width],
                                                      MIDS[j], None, op0=ALU.is_gt))
                else:
                    ve(lambda e, j=j, w=w: e.tensor_scalar(
                        xsq[:, 0:width], qa[:, 0:width], MIDS[j], w,
                        op0=ALU.is_gt, op1=ALU.mult))
                ve(lambda e: e.tensor_add(xnq[:, 0:width], xnq[:, 0:width],
                                          xsq[:, 0:width]))
            ve(lambda e: e.tensor_mul(xnq[:, 0:width], xnq[:, 0:width], qb[:, 0:width]))
            ve(lambda e: e.tensor_mul(
                qo[:, 0:width].rearrange("p (g b) -> p g b", b=16),
                xnq[:, 0:width].rearrange("p (g b) -> p g b", b=16),
                qc[:, 3 * nblk:4 * nblk].unsqueeze(2).broadcast_to([128, nblk, 16])))

        for wsrc, wdst in ((wg, sc_wgq), (wu, sc_wuq)):
            for t in range(22):
                dma1(xnq[:, 0:1024], wsrc[128 * t:128 * (t + 1), :])
                quant_tile(1024)
                dma1(wdst[128 * t:128 * (t + 1), :], qo[:, 0:1024])
        for t in range(8):
            for p0, w in ((0, 1024), (1024, 1024), (2048, 768)):
                dma1(xnq[:, 0:w], wd[128 * t:128 * (t + 1), p0:p0 + w])
                quant_tile(w)
                dma1(sc_wdq[128 * t:128 * (t + 1), p0:p0 + w], qo[:, 0:w])

        def front_proj(c0, T, dst0, zc0, do_z, nf):
            for k in range(8):
                dma2(xTb[:, k * 512:k * 512 + T], xT[128 * k:128 * (k + 1), c0:c0 + T])

            def mms(e):
                ins = None
                for k in range(8):
                    ins = e.matmul(pp[0:1, 512:512 + T], onecol[:, :],
                                   xTb[:, k * 512:k * 512 + T],
                                   start=(k == 0), stop=(k == 7))
                return ins
            pe(mms, deps=("d2",))
            for k in range(8):
                se(lambda e, k=k: e.square(xsq[:, 0:T], xTb[:, k * 512:k * 512 + T]),
                   deps=("d2", "pe"))
                ve(lambda e: e.tensor_copy(qo[:, 0:T], xsq[:, 0:T]), deps=("se",))

                def mks(e, k=k):
                    return e.matmul(pp[32:33, 512:512 + T], onecol[:, :], qo[:, 0:T],
                                    start=(k == 0), stop=(k == 7))
                pe(mks, deps=("ve",))
            ve(lambda e: e.tensor_scalar(rows[0:1, 0:T], pp[0:1, 512:512 + T],
                                         1.0 / DM, None, op0=ALU.mult), deps=("pe",))
            ve(lambda e: e.tensor_scalar(rows[0:1, 512:512 + T], rows[0:1, 0:T],
                                         -1.0, None, op0=ALU.mult))
            ve(lambda e: e.tensor_mul(rows[0:1, 1024:1024 + T], rows[0:1, 0:T],
                                      rows[0:1, 0:T]))
            ve(lambda e: e.tensor_scalar(xsq[0:1, 0:T], pp[32:33, 512:512 + T],
                                         1.0 / DM, None, op0=ALU.mult))
            ve(lambda e: e.tensor_sub(rows[0:1, 1536:1536 + T], xsq[0:1, 0:T],
                                      rows[0:1, 1024:1024 + T]))
            se(lambda e: e.activation(rows[0:1, 1024:1024 + T], rows[0:1, 1536:1536 + T],
                                      AF.Sqrt, bias=epsb[0:1, 0:1], scale=1.0),
               deps=("ve",))
            ve(lambda e: e.reciprocal(rows[0:1, 1536:1536 + T], rows[0:1, 1024:1024 + T]),
               deps=("se",))
            ve(lambda e: e.tensor_copy(negmb[0:1, 0:T], rows[0:1, 512:512 + T]))
            pe(lambda e: e.matmul(pp[0:128, 0:T], ones128r[0:1, :],
                                  rows[0:1, 1536:1536 + T], start=True, stop=True),
               deps=("ve",))
            ve(lambda e: e.tensor_copy(qb[:, 0:T], pp[:, 0:T]), deps=("pe",))
            for f in range(nf):
                def mkf(f=f):
                    def g(e):
                        ins = None
                        for k in range(8):
                            ins = e.matmul(
                                mp[:, 0:T],
                                winb[:, k * 4096 + f * 128:k * 4096 + (f + 1) * 128],
                                xTb[:, k * 512:k * 512 + T],
                                start=(k == 0), stop=False)
                        return e.matmul(mp[:, 0:T], wrow[0:1, f * 128:(f + 1) * 128],
                                        negmb[0:1, 0:T], start=False, stop=True)
                    return g
                pe(mkf(), deps=("ve",))
                if f < 16:
                    ve(lambda e, f=f: e.tensor_mul(
                        xinb[:, f * 515 + dst0:f * 515 + dst0 + T],
                        mp[:, 0:T], qb[:, 0:T]), deps=("pe",))
                else:
                    ve(lambda e: e.tensor_mul(zstg[:, 0:T], mp[:, 0:T], qb[:, 0:T]),
                       deps=("pe",))
                    dma2(sc_z[128 * (f - 16):128 * (f - 15), zc0:zc0 + T],
                         zstg[:, 0:T])
                pe(lambda e: e.nop(), deps=("ve",))

        def front_seg(s):
            S0 = s * SEG
            if s == 0:
                front_proj(0, 3, 0, 0, do_z=False, nf=16)
            else:
                ve(lambda e: e.tensor_copy(
                    xinb.ap().rearrange("p (f c) -> p f c", c=515)[:, :, 0:3],
                    estash.ap().rearrange("p (f c) -> p f c", c=3)))
            front_proj(3 + S0, SEG, 3, S0, do_z=True, nf=32)
            for j in range(4):
                dma1(qa[:, 0:1024], xtm[S0 + 128 * j:S0 + 128 * (j + 1), :],
                     deps=("ve", "se", "pe", "d1"))
                xv = qa[:, 0:1024]
                ve(lambda e, xv=xv, j=j: e.reduce_sum(colb[:, 0:1], xv, axis=AX.X),
                   deps=("d1",))
                se(lambda e, xv=xv: e.square(xsq[:, 0:1024], xv))
                ve(lambda e: e.reduce_sum(colb[:, 1:2], xsq[:, 0:1024], axis=AX.X),
                   deps=("se",))
                ve(lambda e: e.tensor_scalar(colb[:, 2:3], colb[:, 0:1], 1.0 / DM,
                                             None, op0=ALU.mult))
                ve(lambda e: e.tensor_mul(colb[:, 3:4], colb[:, 2:3], colb[:, 2:3]))
                ve(lambda e: e.tensor_scalar(colb[:, 4:5], colb[:, 1:2], 1.0 / DM,
                                             None, op0=ALU.mult))
                ve(lambda e: e.tensor_sub(colb[:, 5:6], colb[:, 4:5], colb[:, 3:4]))
                se(lambda e: e.activation(colb[:, 6:7], colb[:, 5:6], AF.Sqrt,
                                          bias=epsb[:, 0:1], scale=1.0), deps=("ve",))
                ve(lambda e: e.reciprocal(colb[:, 7:8], colb[:, 6:7]), deps=("se",))
                ve(lambda e, xv=xv: e.tensor_scalar(
                    xnq[:, 0:1024], xv, colb[:, 2:3], colb[:, 7:8],
                    op0=ALU.subtract, op1=ALU.mult))
                quant_tile(1024)
                dma1(sc_xq[S0 + 128 * j:S0 + 128 * (j + 1), :], qo[:, 0:1024])
            ve(lambda e: e.tensor_copy(
                estash.ap().rearrange("p (f c) -> p f c", c=3),
                xinb.ap().rearrange("p (f c) -> p f c", c=515)[:, :, 512:515]))
            for f in range(16):
                dma2(convdbs[:, :], convd[:, f * 512:(f + 1) * 512], deps=("pe",))
                def mkc(f=f):
                    def g(e):
                        ins = None
                        for k in range(4):
                            ins = e.matmul(
                                mp[:, 0:512],
                                convdbs[:, k * 128:(k + 1) * 128],
                                xinb[:, f * 515 + k:f * 515 + k + 512],
                                start=(k == 0), stop=(k == 3))
                        return ins
                    return g
                pe(mkc(), deps=("ve", "d2"))
                se(lambda e, f=f: e.activation(xinb[:, f * 515 + 3:f * 515 + 515],
                                               mp[:, 0:512], AF.Silu,
                                               bias=convbc[:, f:f + 1], scale=1.0),
                   deps=("pe",))
                pe(lambda e: e.nop(), deps=("se",))
                dma2(sc_xc[128 * f:128 * (f + 1), S0:S0 + SEG],
                     xinb[:, f * 515 + 3:f * 515 + 515])
                if s == 0 and f == 0:
                    dma1(dbg[:, 1024:1536], xinb[:, 3:515])
            def mb(e):
                ins = None
                for c in range(16):
                    ins = e.matmul(pp[0:64, 0:512], wxb[:, c * 129:c * 129 + 64],
                                   xinb[:, c * 515 + 3:c * 515 + 515],
                                   start=(c == 0), stop=(c == 15))
                return ins
            pe(mb, deps=("se",))
            ve(lambda e: e.tensor_copy(bb[:, :], pp[0:64, 0:512]), deps=("pe",))

            def mc(e):
                ins = None
                for c in range(16):
                    ins = e.matmul(pp[64:128, 0:512], wxb[:, c * 129 + 64:c * 129 + 128],
                                   xinb[:, c * 515 + 3:c * 515 + 515],
                                   start=(c == 0), stop=(c == 15))
                return ins
            pe(mc, deps=("se",))
            ve(lambda e: e.tensor_copy(cb[:, :], pp[64:128, 0:512]), deps=("pe",))

            def md(e):
                ins = None
                for c in range(16):
                    ins = e.matmul(pp[0:1, 512:1024], wxb[:, c * 129 + 128:c * 129 + 129],
                                   xinb[:, c * 515 + 3:c * 515 + 515],
                                   start=(c == 0), stop=(c == 15))
                return ins
            pe(md, deps=("ve",))
            se(lambda e: e.activation(rows[0:1, 0:512], pp[0:1, 512:1024], AF.Abs),
               deps=("pe",))
            se(lambda e: e.activation(rows[0:1, 512:1024], rows[0:1, 0:512], AF.Exp,
                                      bias=0.0, scale=-1.0))
            ve(lambda e: e.tensor_scalar(rows[0:1, 512:1024], rows[0:1, 512:1024],
                                         1.0, None, op0=ALU.add), deps=("se",))
            se(lambda e: e.activation(rows[0:1, 1024:1536], rows[0:1, 512:1024], AF.Ln),
               deps=("ve",))
            ve(lambda e: e.tensor_scalar(dtrow[0:1, :], pp[0:1, 512:1024], 0.0, None,
                                         op0=ALU.max), deps=("se",))
            ve(lambda e: e.tensor_add(dtrow[0:1, :], dtrow[0:1, :],
                                      rows[0:1, 1024:1536]))
            pe(lambda e: e.nop(), deps=("ve",))

        def mid_seg(s):
            S0 = s * SEG
            dma1(sc_mid[0:1, 0:512], dtrow[0:1, :])
            dma1(dtcm[:, :], sc_mid[0:1, 0:512].rearrange("o (c t) -> (o c) t", t=64),
                 deps=("d1",))
            ve(lambda e: e.memset(t2cm[:, :], 0.0), deps=("d1",), drain=False)
            P.step("vector", lambda e: e.drain())
            ve(lambda e: e.tensor_tensor_scan(tlcm[:, :], dtcm[:, :], t2cm[:, :], 0.0,
                                              op0=ALU.add, op1=ALU.add), deps=("d1",))
            ve(lambda e: e.tensor_copy(t2cm[:, :], tlcm[:, :]))
            ve(lambda e: e.tensor_scalar(tlcm[:, 32:64], t2cm[:, 32:64],
                                         t2cm[:, 31:32], None, op0=ALU.subtract))
            ve(lambda e: e.tensor_scalar(rtlcm[:, 0:32], tlcm[:, 0:32], -1.0,
                                         tlcm[:, 31:32], op0=ALU.mult, op1=ALU.add))
            ve(lambda e: e.tensor_scalar(rtlcm[:, 32:64], tlcm[:, 32:64], -1.0,
                                         tlcm[:, 63:64], op0=ALU.mult, op1=ALU.add))
            ve(lambda e: e.tensor_sub(
                dlcm.ap().rearrange("p (ti a t) -> p ti a t", a=32, t=32),
                tlcm.ap().rearrange("p (ti t) -> p ti t", t=32).unsqueeze(2)
                .broadcast_to([8, 2, 32, 32]),
                tlcm.ap().rearrange("p (ti a) -> p ti a", a=32).unsqueeze(3)
                .broadcast_to([8, 2, 32, 32])))
            dma1(sc_mid[0:1, 1024:1536].rearrange("o (p t) -> (o p) t", t=64),
                 tlcm[:, :])
            dma1(sc_mid[0:1, 1536:2048].rearrange("o (p t) -> (o p) t", t=64),
                 rtlcm[:, :])
            dma1(sc_mid[0:1, 2048:18432].rearrange("o (p t) -> (o p) t", t=2048),
                 dlcm[:, :])
            dma1(sc_mid[0:1, 18432:18448].rearrange("o (p a) -> (o p) a", a=2).unsqueeze(2),
                 tlcm.ap().rearrange("p (a t) -> p a t", t=32)[:, :, 31:32],
                 allow_slow_non_contiguous=True)
            dma1(tlrow[0:1, :], sc_mid[0:1, 1024:1536], deps=("d1",))
            dma1(rtlrow[0:1, :], sc_mid[0:1, 1536:2048], deps=("d1",))
            dma1(strow[0:1, :], sc_mid[0:1, 18432:18448], deps=("d1",))
            pe(lambda e: e.matmul(mp[0:64, 0:512], arow[:, :], tlrow[0:1, :],
                                  start=True, stop=True), deps=("d1",))
            se(lambda e: e.activation(einb[:, :], mp[0:64, 0:512], AF.Exp),
               deps=("pe",))
            pe(lambda e: e.matmul(mp[0:64, 0:512], arow[:, :], rtlrow[0:1, :],
                                  start=True, stop=True), deps=("se",))
            se(lambda e: e.activation(eodb[:, :], mp[0:64, 0:512], AF.Exp),
               deps=("pe",))
            pe(lambda e: e.matmul(mp[0:64, 0:512], onerow64[:, :], dtrow[0:1, :],
                                  start=True, stop=True), deps=("se",))
            ve(lambda e: e.tensor_mul(eodb[:, :], eodb[:, :], mp[0:64, 0:512]),
               deps=("pe",))
            ve(lambda e: e.tensor_mul(bdt[:, :], bb[:, :], mp[0:64, 0:512]))
            ve(lambda e: e.tensor_copy(cbb[:, :], cb[:, :]))
            pe(lambda e: e.matmul(mp[0:64, 0:16], arow[:, :], strow[0:1, :],
                                  start=True, stop=True), deps=("ve",))
            se(lambda e: e.activation(lamt[:, :], mp[0:64, 0:16], AF.Exp), deps=("pe",))
            ve(lambda e: e.tensor_mul(lamp[:, :], lamt.ap()[:, 0:16:2],
                                      lamt.ap()[:, 1:16:2]), deps=("se",))
            ve(lambda e: e.tensor_mul(bt[:, :], bb[:, :], eodb[:, :]))
            ve(lambda e: e.tensor_mul(ce[:, :], cb[:, :], einb[:, :]))
            for p in range(8):
                ve(lambda e, p=p: e.tensor_copy(cx[:, 64 * p:64 * p + 32],
                                                ce[:, 64 * p:64 * p + 32]), drain=False)
                ve(lambda e, p=p: e.tensor_scalar(
                    cx[:, 64 * p + 32:64 * p + 64], ce[:, 64 * p + 32:64 * p + 64],
                    lamt[:, 2 * p:2 * p + 1], None, op0=ALU.mult))
                ve(lambda e, p=p: e.tensor_scalar(
                    cedstg[:, :], cx[:, 64 * p:64 * (p + 1)],
                    rdc[:, 0:1], None, op0=ALU.mult))
                dma1(sc_ced[:, S0 + 64 * p:S0 + 64 * (p + 1)], cedstg[:, :])
                ve(lambda e, p=p: e.tensor_mul(rdc[:, 0:1], rdc[:, 0:1],
                                               lamp[:, p:p + 1]))
                ve(lambda e, p=p: e.tensor_scalar(
                    wfm[:, 64 * p:64 * p + 32], bt[:, 64 * p:64 * p + 32],
                    lamt[:, 2 * p + 1:2 * p + 2], None, op0=ALU.mult), drain=False)
                ve(lambda e, p=p: e.tensor_copy(wfm[:, 64 * p + 32:64 * p + 64],
                                                bt[:, 64 * p + 32:64 * p + 64]))
            dma1(sc_W[:, :], wfm[:, :])
            for g in range(4):
                dma1t(wtm[:, g * 64:(g + 1) * 64],
                      sc_W[:, 128 * g:128 * (g + 1)], deps=("d1",))
            for p in range(8):
                if p % 2 == 0:
                    g = p // 2
                    dma1t(xctm[:, 0:2048],
                          sc_xc[:, S0 + 128 * g:S0 + 128 * (g + 1)],
                          deps=("d1", "pe"))
                chunk_64(s, p)

        def chunk_64(s, p):
            S0 = s * SEG
            glob = s * 8 + p
            g2 = 0
            roff = 64 * (p % 2)
            dma1(dlrow[0:1, :], sc_mid[0:1, 2048 + p * 2048:2048 + (p + 1) * 2048],
                 deps=("d1", "pe"))
            for ti in range(2):
                def mkp(ti=ti, p=p):
                    def g(e):
                        base = ti * 1024
                        e.matmul(pp[0:64, 0:512], arow[:, :],
                                 dlrow[0:1, base:base + 512], start=True, stop=False)
                        e.matmul(pp[0:64, 0:512], onerow64[:, :], maskb[0:1, 0:512],
                                 start=False, stop=True)
                        e.matmul(pp[0:64, 512:1024], arow[:, :],
                                 dlrow[0:1, base + 512:base + 1024], start=True,
                                 stop=False)
                        return e.matmul(pp[0:64, 512:1024], onerow64[:, :],
                                        maskb[0:1, 512:1024], start=False, stop=True)
                    return g
                pe(mkp(), deps=("ve", "d1", "se"))
                se(lambda e: e.activation(ebuf[:, :], pp[0:64, 0:1024], AF.Exp),
                   deps=("pe",))
                tb = 64 * p + 32 * ti
                ve(lambda e, tb=tb: e.tensor_mul(
                    ebuf.ap().rearrange("p (a t) -> p a t", t=32),
                    ebuf.ap().rearrange("p (a t) -> p a t", t=32),
                    bdt[:, tb:tb + 32].unsqueeze(2).broadcast_to([64, 32, 32])),
                   deps=("se",))
                ve(lambda e, tb=tb: e.tensor_mul(
                    qbufb.ap().rearrange("p (a t) -> p a t", t=32),
                    ebuf.ap().rearrange("p (a t) -> p a t", t=32),
                    cbb[:, tb:tb + 32].unsqueeze(1).broadcast_to([64, 32, 32])))
                def mkr(e):
                    e.matmul(pp[0:1, 0:512], onecol64[:, :], qbufb[:, 0:512],
                             start=True, stop=True)
                    return e.matmul(pp[0:1, 512:1024], onecol64[:, :],
                                    qbufb[:, 512:1024], start=True, stop=True)
                pe(mkr, deps=("ve",))
                se(lambda e: e.activation(mrow[0:1, :], pp[0:1, 0:1024], AF.Identity)
                   if False else e.copy(mrow[0:1, :], pp[0:1, 0:1024]), deps=("pe",))
                dma1(sc_M[0:1, :], mrow[0:1, :])
                dma1(mpair[roff + 32 * ti:roff + 32 * ti + 32, 32 * ti:32 * ti + 32],
                     sc_M[0:1, :].rearrange("o (a t) -> (o a) t", t=32), deps=("d1",))
            pe(lambda e, p=p: e.matmul(mp[0:32, 0:32], bt[:, 64 * p:64 * p + 32],
                                       ce[:, 64 * p + 32:64 * p + 64],
                                       start=True, stop=True), deps=("ve", "d1"))
            se(lambda e: e.copy(modb[:, :], mp[0:32, 0:32]), deps=("pe",))
            dma2(mpair[roff:roff + 32, 32:64], modb[:, :], deps=("se",))
            for dt_ in range(4):
                def mk2(dt_=dt_, p=p, g2=g2, roff=roff, glob=glob):
                    def g(e):
                        ins = e.matmul(yp[0:64, 0:512], mpair[roff:roff + 64, :],
                                       xctm[roff:roff + 64,
                                            g2 * 2048 + dt_ * 512:g2 * 2048 + (dt_ + 1) * 512],
                                       start=True, stop=(glob == 0))
                        if glob > 0:
                            e.matmul(yp[0:32, 0:512], ce[:, 64 * p:64 * p + 32],
                                     hsb[:, dt_ * 512:(dt_ + 1) * 512],
                                     start=False, stop=False)
                            ins = e.matmul(yp[32:64, 0:512],
                                           cx[:, 64 * p + 32:64 * p + 64],
                                           hsb[:, dt_ * 512:(dt_ + 1) * 512],
                                           start=False, stop=True)
                        return ins
                    return g
                pe(mk2(), deps=("d2", "ve", "d1"))
                ve(lambda e: e.tensor_copy(ytm[:, 0:512], yp[0:64, 0:512]),
                   deps=("pe",))
                pe(lambda e: e.nop(), deps=("ve",))
                dma2(sc_y[S0 + 64 * p:S0 + 64 * (p + 1), dt_ * 512:(dt_ + 1) * 512],
                     ytm[:, 0:512])
            def mks(p=p, g2=g2, roff=roff):
                def g(e):
                    ins = None
                    for dt_ in range(4):
                        ins = e.matmul(hp[:, dt_ * 512:(dt_ + 1) * 512],
                                       wtm[roff:roff + 64,
                                           (p // 2) * 64:(p // 2) * 64 + 64],
                                       xctm[roff:roff + 64,
                                            g2 * 2048 + dt_ * 512:g2 * 2048 + (dt_ + 1) * 512],
                                       start=True, stop=True)
                    return ins
                return g
            pe(mks(), deps=("ve", "d1"))
            if glob > 0:
                ve(lambda e, p=p: e.scalar_tensor_tensor(
                    hfb[:, :], hfb[:, :], lamp[:, p:p + 1], hp[:, :],
                    op0=ALU.mult, op1=ALU.add), deps=("pe",))
            else:
                ve(lambda e: e.tensor_copy(hfb[:, :], hp[:, :]), deps=("pe",))
            ve(lambda e: e.tensor_copy(hsb[:, :], hfb[:, :]))
            pe(lambda e: e.nop(), deps=("ve",))

        for s in range(NSEG):
            front_seg(s)
            mid_seg(s)

        for j in range(4):
            ve(lambda e, j=j: e.tensor_scalar(ccst[:, :], hfb[:, :],
                                              sselb[:, j:j + 1], None, op0=ALU.mult))
            dma1(cc_in[64 * j:64 * (j + 1), :], ccst[:, :])
        P.step("gpsimd",
               lambda e: e.collective_compute(
                   "AllReduce", ALU.add, replica_groups=[list(range(8))],
                   ins=[cc_in.opt()], outs=[cc_out.opt()]),
               deps=("d1",), inc="cc", n=1)

        barrier()
        es1.close()
        es2 = ExitStack()
        E2 = es2.enter_context

        def S2(name, shape, dt=F32):
            return E2(nc.sbuf_tensor(name, shape, dt))

        xqTs = S2("xqTs", [128, 8 * 4096], BF16)
        wstgA = S2("wstgA", [128, 8 * 512], BF16)
        wstgB = S2("wstgB", [128, 8 * 512], BF16)
        gsilA = S2("gsilA", [128, 512])

        for k in range(8):
            dma1t(xqTs[:, k * 4096:(k + 1) * 4096],
                  sc_xq[:, 128 * k:128 * (k + 1)], deps=("d1",))
        NT = SPAN // 128
        for nchunk in range(6):
            n0 = nchunk * 512
            nw = min(512, IF - n0)
            for k in range(8):
                dma1t(wstgA[:, k * 512:k * 512 + nw],
                      sc_wgq[n0:n0 + nw, 128 * k:128 * (k + 1)], deps=("d1", "pe"))
                dma1t(wstgB[:, k * 512:k * 512 + nw],
                      sc_wuq[n0:n0 + nw, 128 * k:128 * (k + 1)], deps=("d1", "pe"))
            for t in range(NT):
                def mg(t=t, nw=nw):
                    def g(e):
                        ins = None
                        for k in range(8):
                            ins = e.matmul(yp[:, 0:nw],
                                           xqTs[:, k * 4096 + t * 128:k * 4096 + (t + 1) * 128],
                                           wstgA[:, k * 512:k * 512 + nw],
                                           start=(k == 0), stop=(k == 7))
                        return ins
                    return g
                pe(mg(), deps=("d1",))
                se(lambda e, nw=nw: e.activation(gsilA[:, 0:nw], yp[:, 0:nw], AF.Silu),
                   deps=("pe",))

                def mu(t=t, nw=nw):
                    def g(e):
                        ins = None
                        for k in range(8):
                            ins = e.matmul(mp[:, 0:nw],
                                           xqTs[:, k * 4096 + t * 128:k * 4096 + (t + 1) * 128],
                                           wstgB[:, k * 512:k * 512 + nw],
                                           start=(k == 0), stop=(k == 7))
                        return ins
                    return g
                pe(mu(), deps=("se",))
                ve(lambda e, nw=nw: e.tensor_mul(xnq[:, 0:nw], gsilA[:, 0:nw],
                                                 mp[:, 0:nw]), deps=("pe", "se"))
                pe(lambda e: e.nop(), deps=("ve",))
                quant_tile(nw)
                dma2(sc_hq[128 * t:128 * (t + 1), n0:n0 + nw], qo[:, 0:nw])
                if t == 0 and nchunk == 0:
                    dma1(dbg[:, 1536:2048], qo[:, 0:512])
                    dma2(dbg[:, 3072:3584], xnq[:, 0:512])

        barrier()
        es2.close()
        es3 = ExitStack()
        E3 = es3.enter_context

        def S3(name, shape, dt=F32):
            return E3(nc.sbuf_tensor(name, shape, dt))

        woutb = S3("woutb", [128, 16 * 1024], BF16)
        wdqb = S3("wdqb", [128, 22 * 1024], BF16)
        hqTs = S3("hqTs", [128, 22 * 512], BF16)
        ygfm = S3("ygfm", [128, 16 * 512], BF16)
        yfmt = S3("yfmt", [128, 512], BF16)
        xctb = S3("xctb", [128, 512], BF16)
        zstg2 = S3("zstg2", [128, 512], BF16)
        gsilB = S3("gsilB", [128, 512])
        hrecv = S3("hrecv", [64, 2048])
        hrecvb = S3("hrecvb", [64, 2048], BF16)
        outsb = xnq
        xres = qa
        cedt = S3("cedt", [64, 512], BF16)

        ve(lambda e: e.memset(hrecv[:, :], 0.0), deps=("cc",))
        for j in range(4):
            dma1(ccst[:, :], cc_out[64 * j:64 * (j + 1), :], deps=("cc", "ve"))
            ve(lambda e, j=j: e.tensor_scalar(hfb[:, :], ccst[:, :],
                                              rselb[:, j:j + 1], None, op0=ALU.mult),
               deps=("d1",))
            ve(lambda e: e.tensor_add(hrecv[:, :], hrecv[:, :], hfb[:, :]))
        ve(lambda e: e.tensor_copy(hrecvb[:, :], hrecv[:, :]))

        for k in range(16):
            dma2(woutb[:, k * 1024:(k + 1) * 1024], woutT[128 * k:128 * (k + 1), :],
                 deps=("pe", "ve"))
        for i in range(22):
            dma1t(wdqb[:, i * 1024:(i + 1) * 1024],
                  sc_wdq[:, 128 * i:128 * (i + 1)], deps=("d1", "pe"))
        for s in range(NSEG):
            S0 = s * SEG
            dma1(cedt[:, :], sc_ced[:, S0:S0 + SEG], deps=("ve", "pe"))
            for f in range(16):
                dma1t(yfmt[:, :], sc_y[S0:S0 + SEG, 128 * f:128 * (f + 1)],
                      deps=("d1", "ve"))
                dma1(xctb[:, :], sc_xc[128 * f:128 * (f + 1), S0:S0 + SEG])
                dma1(zstg2[:, :], sc_z[128 * f:128 * (f + 1), S0:S0 + SEG])

                pe(lambda e, f=f: e.matmul(mp[:, 0:512],
                                           hrecvb[:, f * 128:(f + 1) * 128],
                                           cedt[:, :],
                                           start=True, stop=True),
                   deps=("ve", "d1"))
                ve(lambda e: e.tensor_copy(qa[:, 0:512], yfmt[:, :]), deps=("d1",))
                ve(lambda e: e.tensor_add(qa[:, 0:512], qa[:, 0:512], mp[:, 0:512]),
                   deps=("pe",))
                pe(lambda e: e.nop(), deps=("ve",))
                ve(lambda e, f=f: e.scalar_tensor_tensor(
                    qb[:, 0:512], xctb[:, :], dcolb[:, f:f + 1], qa[:, 0:512],
                    op0=ALU.mult, op1=ALU.add), deps=("d1",))
                se(lambda e: e.activation(gsilB[:, 0:512], zstg2[:, :], AF.Silu),
                   deps=("d1",))
                ve(lambda e, f=f: e.tensor_mul(ygfm[:, f * 512:(f + 1) * 512],
                                               qb[:, 0:512], gsilB[:, 0:512]),
                   deps=("se",))
            for i in range(22):
                dma1t(hqTs[:, i * 512:(i + 1) * 512],
                      sc_hq[S0:S0 + SEG, 128 * i:128 * (i + 1)], deps=("d1", "pe"))
            for t4 in range(4):
                T0 = S0 + 128 * t4
                dma1(xres[:, :], xtm[T0:T0 + 128, :])
                for nh in range(2):
                    def mo(t4=t4, nh=nh):
                        def g(e):
                            ins = None
                            for k in range(16):
                                ins = e.matmul(
                                    yp[:, 0:512],
                                    ygfm[:, (k // 4) * 2048 + 0:0] if False else
                                    ygfm[:, k * 512 + 0:k * 512 + 0] if False else
                                    ygfm.ap().rearrange("p (f t) -> p f t", t=512)
                                    [:, k, 128 * t4:128 * t4 + 128],
                                    woutb[:, k * 1024 + nh * 512:k * 1024 + (nh + 1) * 512],
                                    start=(k == 0), stop=False)
                            for i in range(22):
                                ins = e.matmul(
                                    yp[:, 0:512],
                                    hqTs[:, i * 512 + 128 * t4:i * 512 + 128 * (t4 + 1)],
                                    wdqb[:, i * 1024 + nh * 512:i * 1024 + (nh + 1) * 512],
                                    start=False, stop=(i == 21))
                            return ins
                        return g
                    pe(mo(), deps=("ve", "d1", "d2"))
                    ve(lambda e, nh=nh: e.tensor_add(outsb[:, nh * 512:(nh + 1) * 512],
                                                     xres[:, nh * 512:(nh + 1) * 512],
                                                     yp[:, 0:512]), deps=("pe", "d1"))
                    pe(lambda e: e.nop(), deps=("ve",))
                dma1(out[T0:T0 + 128, :], outsb[:, :])

        P.step("sync", lambda e: e.wait_ge(P.sem["d1"], P.cnt["d1"]))
        P.step("gpsimd", lambda e: e.wait_ge(P.sem["d2"], P.cnt["d2"]))

        @blk.sync
        def _(eng):
            for en, fn in P.steps:
                if en == "sync":
                    fn(eng)

        @blk.vector
        def _(eng):
            for en, fn in P.steps:
                if en == "vector":
                    fn(eng)

        @blk.scalar
        def _(eng):
            for en, fn in P.steps:
                if en == "scalar":
                    fn(eng)

        @blk.tensor
        def _(eng):
            for en, fn in P.steps:
                if en == "tensor":
                    fn(eng)

        @blk.gpsimd
        def _(eng):
            for en, fn in P.steps:
                if en == "gpsimd":
                    fn(eng)

        es3.close()

    return nc


def stage_inputs(inputs, c):
    b, h = c // 2, c % 2
    x = np.asarray(inputs["L0_attn_x"])
    f32 = np.float32
    xb = np.ascontiguousarray(x[b])
    s0 = h * SPAN
    xTfull = np.ascontiguousarray(xb.T)
    halo = np.zeros((DM, 3), f32) if h == 0 else xTfull[:, s0 - 3:s0]
    xTs = np.concatenate([halo, xTfull[:, s0:s0 + SPAN]], axis=1)
    xtm_s = np.ascontiguousarray(xb[s0:s0 + SPAN])
    cw = np.asarray(inputs["L0_attn_conv_w"]).reshape(DI, 4)
    cdd = np.zeros((4, 16, 128, 128), f32)
    for k in range(4):
        for f in range(16):
            np.fill_diagonal(cdd[k, f], cw[128 * f:128 * (f + 1), k])
    convd_s = np.ascontiguousarray(cdd.transpose(2, 1, 0, 3).reshape(128, 64 * 128))
    convb_s = np.ascontiguousarray(
        np.asarray(inputs["L0_attn_conv_b"]).reshape(16, 128).T)
    dcol_s = np.ascontiguousarray(np.asarray(inputs["L0_attn_D"]).reshape(16, 128).T)
    mr = np.zeros((1, 1024), f32)
    for a in range(32):
        for t in range(32):
            if t < a:
                mr[0, 32 * a + t] = -30000.0
    ss = np.zeros((64, 4), f32)
    rs = np.zeros((64, 4), f32)
    if h == 0:
        ss[:, b] = 1.0
    else:
        rs[:, b] = 1.0
    return {
        "xT": np.ascontiguousarray(xTs, f32),
        "xtm": np.ascontiguousarray(xtm_s, f32),
        "winT": np.ascontiguousarray(np.asarray(inputs["L0_attn_W_in"]).T, f32),
        "wxT": np.ascontiguousarray(np.asarray(inputs["L0_attn_W_x"]).T, f32),
        "convd": convd_s.astype(f32),
        "convb": convb_s.astype(f32),
        "alog": np.asarray(inputs["L0_attn_A_log"], f32).reshape(1, DS),
        "dcol": dcol_s.astype(f32),
        "woutT": np.ascontiguousarray(np.asarray(inputs["L0_attn_W_out"]).T, f32),
        "wg": np.asarray(inputs["L0_ffn_w_gate"], f32),
        "wu": np.asarray(inputs["L0_ffn_w_up"], f32),
        "wd": np.asarray(inputs["L0_ffn_w_down"], f32),
        "maskrow": mr,
        "slotsel": ss,
        "recvsel": rs,
    }


def kernel(**inputs):
    nc = build_program()
    in_maps = [stage_inputs(inputs, c) for c in range(8)]
    res = run_bass_kernel_spmd(nc, in_maps, core_ids=list(range(8)))
    outp = np.zeros((B, L, DM), np.float32)
    for c in range(8):
        b, h = c // 2, c % 2
        outp[b, h * SPAN:(h + 1) * SPAN, :] = np.asarray(res.results[c]["out"])
    return outp
